# version 04
# baseline (speedup 1.0000x reference)
import numpy as np
import ml_dtypes
import concourse.bass as bass
import concourse.mybir as mybir
from concourse.bass_utils import run_bass_kernel_spmd
from contextlib import ExitStack
from collections import deque

F32 = mybir.dt.float32
BF16 = mybir.dt.bfloat16
AF = mybir.ActivationFunctionType
ALU = mybir.AluOpType
AX = mybir.AxisListType

T = 2048
D = 1024
DC = 8
H = 2048
HC = 16
NE = 10
NST = 16
DTR = 64
NSEG = 8
TS = 256
ROWS = 512
NP = 16
TP = 128
PR = 256
EPS = 1e-5

DEBUG_TAPS = ()
DEBUG_DUMP = ()
STAGE_LIMIT = None
DRAIN_MODE = "interleaved"


def _sched():
    s = ["A0", "A1"]
    for i in range(0, 6):
        s += [f"P{2*i}", f"SC{2*i}", f"P{2*i+1}", f"SC{2*i+1}", f"X{i}", f"H{i+2}"]
        s += [f"A{i+2}", f"E{i}"]
    s += ["P12", "SC12", "P13", "SC13", "X6", "E6"]
    s += ["P14", "SC14", "P15", "SC15", "X7", "E7"]
    return s


class Sem:
    def __init__(self, nc, name):
        self._cm = nc.semaphore(name)
        self.h = self._cm.__enter__()
        self.count = 0
        self.name = name


def build():
    nc = bass.Bass("TRN2", target_bir_lowering=False, debug=False)
    last_wait = {}

    def _wait(eng, engname, sem, val):
        if not val or val <= 0:
            return
        key = (engname, sem.name)
        if last_wait.get(key, 0) >= val:
            return
        eng.wait_ge(sem.h, val)
        last_wait[key] = val

    def din(name, shape):
        return nc.dram_tensor(name, list(shape), F32, kind="ExternalInput").ap()

    x_in = din("x", (2, D, T))
    w1f = din("w1f", (NE, DC, 128, H))
    w2f = din("w2f", (NE, DC, 128, H))
    w3f = din("w3f", (NE, HC, 128, D))
    winf = din("winf", (DC, 128, H))
    woutf = din("woutf", (HC, 128, D))
    wxpf = din("wxpf", (DC, 128, 96))
    wdtf = din("wdtf", (DTR, D))
    gwf = din("gwf", (DC, 128, 8))
    b12_in = din("b12", (128, NE * 2 * HC))
    b3_in = din("b3", (NE, D))
    bdt_in = din("bdt", (128, DC))
    alog_in = din("alog", (128, DC * NST))
    dp_in = din("dp", (128, DC))
    bout_in = din("bout", (128, DC))
    kk_in = din("kk", (128, 2 * DC * 4))
    gb_in = din("gb", (8, 1))
    eye_in = din("eye", (128, 128))

    out = nc.dram_tensor("out", [2, D, T], F32, kind="ExternalOutput").ap()

    w1c = nc.dram_tensor("w1c", [NE, DC, 128, H], BF16, kind="Internal").ap()
    w2c = nc.dram_tensor("w2c", [NE, DC, 128, H], BF16, kind="Internal").ap()
    w3c = nc.dram_tensor("w3c", [NE, HC, 128, D], BF16, kind="Internal").ap()
    winc = nc.dram_tensor("winc", [DC, 128, H], BF16, kind="Internal").ap()
    woutc = nc.dram_tensor("woutc", [HC, 128, D], BF16, kind="Internal").ap()
    spill_d = nc.dram_tensor("spill", [NSEG, 128, DC * ROWS], F32, kind="Internal").ap()
    wrow_d = nc.dram_tensor("wrow", [2, 8, ROWS], F32, kind="Internal").ap()
    bc_d = nc.dram_tensor("bcd", [2, 32, 2 * TP], F32, kind="Internal").ap()

    taps = {}
    for tname in DEBUG_TAPS:
        taps[tname] = nc.dram_tensor(
            "tap_" + tname, [128, DC * ROWS], F32, kind="ExternalOutput").ap()

    st = ExitStack()

    sreg = {}

    def sb(name, shape, dt):
        t = st.enter_context(nc.sbuf_tensor(name, list(shape), dt))
        sreg[name] = (t, list(shape), dt)
        return t

    xf = [sb(f"xf{i}", (128, DC * ROWS), F32) for i in range(2)]
    xe = sb("xe", (128, DC * ROWS), F32)
    xn = [sb(f"xn{i}", (128, DC * ROWS), BF16) for i in range(2)]
    xn2 = sb("xn2", (128, DC * 2 * 130), BF16)
    glu = sb("glu", (128, HC * ROWS), BF16)
    s_str = [sb(f"sstr{i}", (128, ROWS), BF16) for i in range(2)]
    tmp_str = sb("tmpstr", (128, ROWS), F32)
    w_eb = [sb(f"web{i}", (128, ROWS), F32) for i in range(2)]
    wall = [sb(f"wall{i}", (NE, ROWS), F32) for i in range(2)]
    wall_bf = [sb("wallbf0", (NE, ROWS), BF16)] * 2
    NGB = 6
    gpool = [sb(f"gwb{i}", (128, 2048), BF16) for i in range(NGB)]
    wxp_sb = sb("wxp", (128, DC * 96), BF16)
    wdt_sb = sb("wdt", (DTR, D), BF16)
    gw_sb = sb("gwt", (128, DC * 8), BF16)
    xc = [sb(f"xcc{i}", (128, ROWS), BF16) for i in range(2)]
    x2c = [sb(f"x2c{i}", (128, ROWS), BF16) for i in range(2)]
    rows_sb = sb("rows", (1, 1552), F32)
    lgt = sb("lgt", (8, ROWS), F32)
    g_eo = sb("geo", (128, 32), F32)
    srt4 = sb("srt4", (128, 32), F32)
    gmsk = sb("gmsk", (128, 32), F32)
    raw = [sb("raw0", (128, 2 * 131), F32)] * 2
    ca = [sb("cacc0", (128, TP), F32)] * 2
    carry = sb("carry", (128, HC * 2), F32)
    xs = [sb(f"xs{i}", (128, DC * PR), F32) for i in range(2)]
    xsb = [sb("xsb0", (128, PR), BF16)] * 2
    z_bf = sb("zbf", (128, DC * ROWS), BF16)
    y_bf = sb("ybf", (128, DC * ROWS), BF16)
    dtlow = [sb(f"dtlow{i}", (DTR, PR), BF16) for i in range(2)]
    bcst = [sb("bcst0", (32, 2 * TP), F32)] * 2
    delta_s = sb("delta", (128, TP), F32)
    du_s = sb("du", (128, TP), F32)
    NA = 2
    a_pool = [sb(f"apool{i}", (128, TP), F32) for i in range(NA)]
    dbu_b = [sb(f"dbu{i}", (128, TP), F32) for i in range(2)]
    B_sb = sb("Bsb", (128, NST * TP), F32)
    C_sb = sb("Csb", (128, NST * TP), F32)
    h_all = sb("hall", (128, NST * TP), F32)
    h_fin = sb("hfin", (128, DC * NST * 2), F32)
    sp_r = sb("spr", (128, TP), F32)
    sp_a = sb("spa", (128, TP), F32)
    b12_sb = sb("b12s", (128, NE * 2 * HC), F32)
    b3_sb = sb("b3s", (NE, D), BF16)
    A_sb = sb("Asb", (128, DC * NST), F32)
    bdt_sb = sb("bdts", (128, DC), F32)
    dp_sb = sb("dps", (128, DC), F32)
    bout_sb = sb("bouts", (128, DC), F32)
    kk_sb = sb("kks", (128, 2 * DC * 4), F32)
    gb_sb = sb("gbs", (8, 1), F32)
    eye_sb = sb("eyes", (128, 128), F32)
    ones_bf = sb("onesbf", (128, 1), BF16)
    ones_f = sb("onesf", (1, 128), F32)
    onec = sb("onec", (128, 1), F32)
    epsr = sb("epsr", (1, 1), F32)

    pb = [st.enter_context(nc.psum_tensor(f"pb{i}", [128, 512], F32)) for i in range(8)]

    d_g = [Sem(nc, f"d_g{i}") for i in range(NGB)]
    d_sm = Sem(nc, "d_sm")
    d_c6 = Sem(nc, "d_c6")
    d_ce = [Sem(nc, f"d_ce{e}") for e in range(NE)]
    d_in = [Sem(nc, "d_in0"), Sem(nc, "d_in1")]
    d_sp = Sem(nc, "d_sp")
    d_rl = Sem(nc, "d_rl")
    d_out = Sem(nc, "d_out")
    d_web = [Sem(nc, "d_web0"), Sem(nc, "d_web1")]
    d_bco = [Sem(nc, "d_bco0"), Sem(nc, "d_bco1")]
    d_bcl = Sem(nc, "d_bcl")
    d_tap = Sem(nc, "d_tap")
    pe_s = Sem(nc, "pe_s")
    dve_s = Sem(nc, "dve_s")
    act_s = Sem(nc, "act_s")
    gp_s = Sem(nc, "gp_s")
    sems = (d_g + [d_sm, d_c6] + d_ce + d_in +
            [d_sp, d_rl, d_out] + d_web + d_bco + [d_bcl, d_tap, pe_s, dve_s, act_s, gp_s])

    PE, SYNC = nc.tensor, nc.sync

    def _ap_span(x):
        try:
            t = x.tensor
            name = t.name
            off = x.offset
            hi = off + 1
            for (step, cnt) in list(x.ap)[1:]:
                hi += abs(step) * (cnt - 1)
            return (name, off, hi)
        except Exception:
            try:
                return (x.tensor.name, 0, 1 << 60)
            except Exception:
                return None

    class EngW:

        def __init__(self, eng, name, sem):
            self._eng = eng
            self._name = name
            self._sem = sem
            self._acc = {}

        def wait_ge(self, *a, **k):
            return self._eng.wait_ge(*a, **k)

        def _conflict(self, spans, writing):
            need = 0
            waited = last_wait.get((self._name, self._sem.name), 0)
            for (name, lo, hi) in spans:
                lst = self._acc.get(name)
                if not lst:
                    continue
                keep = []
                for (tok, l2, h2, w2) in lst:
                    if tok <= waited:
                        continue
                    keep.append((tok, l2, h2, w2))
                    if (w2 or writing) and l2 < hi and lo < h2:
                        if tok > need:
                            need = tok
                self._acc[name] = keep
            return need

        def _record(self, spans, is_write):
            tok = self._sem.count
            for (name, lo, hi) in spans:
                self._acc.setdefault(name, []).append((tok, lo, hi, is_write))

        def __getattr__(self, op):
            real = getattr(self._eng, op)

            def f(*a, **k):
                aps = [x for x in list(a) + list(k.values())
                       if hasattr(x, "tensor") or hasattr(x, "ap")]
                spans = [sp for sp in (_ap_span(x) for x in aps) if sp]
                if op in ("dma_start", "indirect_dma_start"):
                    need = self._conflict(spans, True)
                    _wait(self._eng, self._name, self._sem, need)
                    return real(*a, **k)
                wsp = spans[:1]
                rsp = spans[1:]
                need = max(self._conflict(rsp, False), self._conflict(wsp, True))
                _wait(self._eng, self._name, self._sem, need)
                inst = real(*a, **k)
                inst.then_inc(self._sem.h, 1)
                self._sem.count += 1
                self._record(wsp, True)
                self._record(rsp, False)
                return self._sem.count
            return f

    DVE = EngW(nc.vector, "dve", dve_s)
    ACT = EngW(nc.scalar, "act", act_s)
    GP = EngW(nc.gpsimd, "gp", gp_s)

    def inc(inst, sem, n=1):
        if isinstance(inst, int):
            return inst
        inst.then_inc(sem.h, n)
        sem.count += n
        return sem.count

    def wpe(sem, val):
        _wait(PE, "pe", sem, val)

    def wdve(sem, val):
        _wait(nc.vector, "dve", sem, val)

    def wact(sem, val):
        _wait(nc.scalar, "act", sem, val)

    def wsync(sem, val):
        _wait(SYNC, "sync", sem, val)

    def wgp(sem, val):
        _wait(nc.gpsimd, "gp", sem, val)

    class Bank:
        def __init__(self, i):
            self.i = i
            self.t = pb[i]
            self.rtoks = []

        def acquire(self):
            for (sem, val) in self.rtoks:
                wpe(sem, val)
            self.rtoks = []

        def done(self, inst):
            return inc(inst, pe_s)

        def reader(self, sem, val):
            self.rtoks.append((sem, val))

    banks = [Bank(i) for i in range(8)]
    util_rr = [0]

    def util_bank():
        b = banks[6 + util_rr[0] % 2]
        util_rr[0] += 1
        return b

    gp_rtoks = [[] for _ in range(NGB)]
    gp_ltok = [0] * NGB
    gp_i = [0]

    gp_resv = [False] * NGB

    def gload(in_ap, shape3):
        for _ in range(NGB):
            slot = gp_i[0] % NGB
            gp_i[0] += 1
            if not gp_resv[slot]:
                break
        else:
            raise RuntimeError(f"granule pool exhausted resv={gp_resv} i={gp_i[0]}")
        gp_resv[slot] = True
        for (sem, val) in gp_rtoks[slot]:
            wsync(sem, val)
        gp_rtoks[slot] = []
        a_, b_ = shape3
        dst = gpool[slot].ap()[:, 0:a_ * b_].rearrange("p (a b) -> p a b", b=b_)
        inst = SYNC.dma_start(out=dst, in_=in_ap.rearrange("a p b -> p a b"))
        gp_ltok[slot] = inc(inst, d_g[slot], 16)
        return slot

    def guse(slot):
        wpe(d_g[slot], gp_ltok[slot])

    def gdone(slot, sem, val):
        if val:
            gp_rtoks[slot].append((sem, val))
            gp_resv[slot] = False

    def sm_dma(out_ap, in_ap):
        return inc(ACT.dma_start(out=out_ap, in_=in_ap), d_sm, 16)

    sm_dma(b12_sb[:], b12_in)
    sm_dma(A_sb[:], alog_in)
    sm_dma(bdt_sb[:], bdt_in)
    sm_dma(dp_sb[:], dp_in)
    sm_dma(bout_sb[:], bout_in)
    sm_dma(kk_sb[:], kk_in)
    sm_dma(gb_sb[:], gb_in)
    t_small = sm_dma(eye_sb[:], eye_in)

    def cast6(out_ap, in_ap):
        return inc(GP.dma_start(out=out_ap, in_=in_ap), d_c6, 16)

    cast6(wxp_sb.ap().rearrange("p (a b) -> p a b", b=96),
          wxpf.rearrange("a p b -> p a b"))
    cast6(wdt_sb[:], wdtf)
    cast6(gw_sb.ap().rearrange("p (a b) -> p a b", b=8),
          gwf.rearrange("a p b -> p a b"))
    cast6(b3_sb[:], b3_in)
    cast6(winc.rearrange("a p b -> (a p b)"), winf.rearrange("a p b -> (a p b)"))
    t_c6 = cast6(woutc.rearrange("a p b -> (a p b)"), woutf.rearrange("a p b -> (a p b)"))
    t_weC = []
    for e in range(NE):
        inc(GP.dma_start(out=w1c[e].rearrange("a p b -> (a p b)"),
                         in_=w1f[e].rearrange("a p b -> (a p b)")), d_ce[e], 16)
        inc(GP.dma_start(out=w2c[e].rearrange("a p b -> (a p b)"),
                         in_=w2f[e].rearrange("a p b -> (a p b)")), d_ce[e], 16)
        t_weC.append(
            inc(GP.dma_start(out=w3c[e].rearrange("a p b -> (a p b)"),
                             in_=w3f[e].rearrange("a p b -> (a p b)")), d_ce[e], 16))

    wact(d_sm, t_small)
    tA = inc(ACT.activation(A_sb[:], A_sb[:], AF.Exp), act_s)
    wdve(act_s, tA)
    DVE.tensor_scalar_mul(A_sb[:], A_sb[:], -1.0)
    DVE.memset(ones_bf[:], 1.0)
    DVE.memset(ones_f[:], 1.0)
    DVE.memset(h_fin[:], 0.0)
    DVE.memset(carry[:], 0.0)
    DVE.memset(wall[0][:], 0.25)
    DVE.memset(onec[:], 1.0)
    DVE.memset(epsr[:], EPS)
    inc(DVE.memset(wall[1][:], 0.25), dve_s)

    x_load_tok = [0] * NSEG
    xseg_done = [0] * NSEG
    spill_tok = [0] * NSEG
    reload_tok = [0] * NSEG
    out_tok = [0] * NSEG
    y_dve_tok = [0] * NP
    z_act_tok = [0] * NP
    ebuf_free = [0]
    y_free_pe = [0]
    z_free_pe = [0]
    xn_free_pe = [0, 0]
    wallbf_free_pe = [0]
    xn2_free_pe = [0]
    xs_free_dve = [0, 0]
    xs_act_tok = [0, 0]
    dtlow_act = [0, 0]
    dtlow_free_pe = [0, 0]
    bc_out = [0, 0]
    bcst_free = [0]
    wrow_free = [0, 0]
    web_used = [0, 0]
    xc_guard = {}
    BC_B_users = [0]
    BC_C_users = [0]
    hall_gp = [0]
    sp_free = [0]
    xsb_guard = [0, 0]
    ca_guard = [0, 0]
    pending = deque()
    sc_done = [False] * NP

    def load_x(i):
        wact(d_tap, d_tap.count)
        t0 = i * TS
        for b in range(2):
            src = x_in[b, :, t0:t0 + TS].rearrange("(dc p) t -> p dc t", p=128)
            dst = xf[i % 2].ap().rearrange("p (dc b t) -> p dc b t", b=2, t=TS)[:, :, b, :]
            x_load_tok[i] = inc(ACT.dma_start(out=dst, in_=src), d_in[i % 2], 16)

    load_x(0)
    load_x(1)

    def emit_ln(groups, ncols, which):
        s0b = util_bank()
        s1b = util_bank()
        s0b.acquire()
        s1b.acquire()
        first = True
        ngr = len(groups)
        for gi_, (gslab, coff, cols) in enumerate(groups):
            for dc in range(DC):
                pp = (which + dc) % 2
                if xc_guard.get(pp):
                    wdve(pe_s, xc_guard[pp])
                src = gslab(dc)
                DVE.tensor_copy(xc[pp][:, 0:cols], src)
                t2 = inc(DVE.tensor_tensor(x2c[pp][:, 0:cols], src, src, ALU.mult), dve_s)
                wpe(dve_s, t2)
                last = (gi_ == ngr - 1 and dc == DC - 1)
                PE.matmul(s0b.t[0:1, coff:coff + cols], ones_bf[:], xc[pp][:, 0:cols],
                          start=first, stop=last)
                inst = PE.matmul(s1b.t[0:1, coff:coff + cols], ones_bf[:],
                                 x2c[pp][:, 0:cols], start=first, stop=last)
                xc_guard[pp] = inc(inst, pe_s)
                first = False
        tstats = xc_guard[(which + DC - 1) % 2]
        m = rows_sb[0:1, 0:ncols]
        r = rows_sb[0:1, 520:520 + ncols]
        nmr = rows_sb[0:1, 1040:1040 + ncols]
        wdve(pe_s, tstats)
        DVE.tensor_scalar_mul(m, s0b.t[0:1, 0:ncols], 1.0 / D)
        DVE.tensor_scalar_mul(r, s1b.t[0:1, 0:ncols], 1.0 / D)
        DVE.tensor_tensor(nmr, m, m, ALU.mult)
        tv = inc(DVE.tensor_tensor(r, r, nmr, ALU.subtract), dve_s)
        s0b.reader(dve_s, tv)
        s1b.reader(dve_s, tv)
        wact(dve_s, tv)
        ACT.activation(r, r, AF.Ln, bias=epsr[:])
        tr = inc(ACT.activation(r, r, AF.Exp, scale=-0.5), act_s)
        wdve(act_s, tr)
        trows = inc(DVE.scalar_tensor_tensor(nmr, m, -1.0, r, ALU.mult, ALU.mult), dve_s)
        rb = util_bank()
        nb = util_bank()
        rb.acquire()
        nb.acquire()
        wpe(dve_s, trows)
        rb.done(PE.matmul(rb.t[:, 0:ncols], ones_f[:], r, start=True, stop=True))
        tb = nb.done(PE.matmul(nb.t[:, 0:ncols], ones_f[:], nmr, start=True, stop=True))
        return rb, nb, tb

    def emit_gate(par, xn_tok):
        xnb = xn[par]
        lb = util_bank()
        lb.acquire()
        wpe(dve_s, xn_tok)
        wpe(d_c6, t_c6)
        for dc in range(DC):
            inst = PE.matmul(lb.t[0:8, 0:ROWS], gw_sb[:, dc * 8:(dc + 1) * 8],
                             xnb[:, dc * ROWS:(dc + 1) * ROWS],
                             start=(dc == 0), stop=(dc == DC - 1))
        tl = lb.done(inst)
        wdve(pe_s, tl)
        wdve(d_sm, t_small)
        tlg = inc(DVE.tensor_scalar(lgt[:], lb.t[0:8, 0:ROWS], gb_sb[:], None, ALU.add),
                  dve_s)
        lb.reader(dve_s, tlg)
        tb = util_bank()
        tb.acquire()
        wpe(dve_s, tlg)
        wpe(d_sm, t_small)
        for blk in range(4):
            inst = PE.transpose(tb.t[:, blk * 8:(blk + 1) * 8],
                                lgt[0:8, blk * 128:(blk + 1) * 128], eye_sb[0:8, 0:8])
        ttp = tb.done(inst)
        wdve(pe_s, ttp)
        DVE.tensor_copy(
            g_eo.ap().rearrange("p (e k) -> p k e", k=4),
            tb.t[:, 0:32].rearrange("p (k e) -> p k e", e=8))
        tge = inc(DVE.tensor_copy(srt4[:], tb.t[:, 0:32]), dve_s)
        tb.reader(dve_s, tge)
        mx = gmsk[:, 0:4]
        DVE.reduce_max(mx, srt4.ap().rearrange("p (k e) -> p k e", e=8), AX.X)
        v_eo = g_eo.ap().rearrange("p (e k) -> p e k", k=4)
        tsb = inc(DVE.tensor_tensor(
            v_eo, v_eo, mx.unsqueeze(1).broadcast_to([128, 8, 4]), ALU.subtract), dve_s)
        wact(dve_s, tsb)
        tex = inc(ACT.activation(g_eo[:], g_eo[:], AF.Exp), act_s)
        wdve(act_s, tex)
        sc1 = gmsk[:, 8:24].rearrange("p (e k) -> p e k", k=4)
        DVE.tensor_tensor(sc1, v_eo[:, 0:4, :], v_eo[:, 4:8, :], ALU.add)
        sc2 = gmsk[:, 24:32].rearrange("p (e k) -> p e k", k=4)
        DVE.tensor_tensor(sc2, sc1[:, 0:2, :], sc1[:, 2:4, :], ALU.add)
        ssum = gmsk[:, 4:8]
        DVE.tensor_tensor(ssum, gmsk[:, 24:28], gmsk[:, 28:32], ALU.add)
        DVE.reciprocal(ssum, ssum)
        DVE.tensor_tensor(v_eo, v_eo, ssum.unsqueeze(1).broadcast_to([128, 8, 4]), ALU.mult)
        vk = g_eo.ap().rearrange("p (e k) -> p k e", k=4)
        for blk in range(4):
            DVE.max(srt4[:, blk * 8:(blk + 1) * 8], vk[:, blk, :])
        m2 = srt4.ap().rearrange("p (k s) -> p s k", s=8)[:, 1, :]
        DVE.tensor_tensor(gmsk.ap().rearrange("p (e k) -> p e k", k=4),
                          v_eo, m2.unsqueeze(1).broadcast_to([128, 8, 4]), ALU.is_ge)
        tw = inc(DVE.scalar_tensor_tensor(
            g_eo[:], g_eo[:], 0.5, gmsk[:], ALU.mult, ALU.mult), dve_s)
        wb = util_bank()
        wb.acquire()
        wpe(dve_s, tw)
        for blk in range(4):
            inst = PE.transpose(wb.t[0:8, blk * 128:(blk + 1) * 128],
                                vk[:, blk, :], eye_sb[:])
        twb = wb.done(inst)
        wdve(pe_s, twb)
        t_wall = inc(DVE.tensor_copy(wall[par][0:8, :], wb.t[0:8, 0:ROWS]), dve_s)
        wb.reader(dve_s, t_wall)
        wdve(pe_s, wallbf_free_pe[0])
        t_wallbf = inc(DVE.tensor_copy(wall_bf[par][:], wall[par][:]), dve_s)
        wact(dve_s, t_wall)
        wact(d_web[0], wrow_free[par])
        wact(d_web[1], wrow_free[par])
        tbo = inc(ACT.dma_start(out=wrow_d[par], in_=wall[par][0:8, :]), d_bco[par], 16)
        return t_wallbf, tbo

    def drain(k):
        for _ in range(k):
            if not pending:
                return
            pending.popleft()()

    head_box = {}

    def emit_fhead(seg, xbuf, kind):
        par = 0 if kind == "A" else 1
        xnb = xn[par]
        which = (seg + (0 if kind == "A" else 1)) % 2
        groups = [(lambda dc, xb=xbuf: xb[:, dc * ROWS:(dc + 1) * ROWS], 0, ROWS)]
        rb, nb, tbc = emit_ln(groups, ROWS, which)
        wdve(pe_s, tbc)
        wdve(pe_s, xn_free_pe[par])
        for dc in range(DC):
            DVE.tensor_tensor(tmp_str[:], xbuf[:, dc * ROWS:(dc + 1) * ROWS],
                              rb.t[:, 0:ROWS], ALU.mult)
            inst = DVE.tensor_tensor(xnb[:, dc * ROWS:(dc + 1) * ROWS],
                                     tmp_str[:], nb.t[:, 0:ROWS], ALU.add)
        t_xn = inc(inst, dve_s)
        rb.reader(dve_s, t_xn)
        nb.reader(dve_s, t_xn)
        t_wallbf, tbo = emit_gate(par, t_xn)
        head_box[(kind, seg)] = (t_xn, t_wallbf, tbo)

    def emit_ffn(seg, xbuf, kind):
        par = 0 if kind == "A" else 1
        xnb = xn[par]
        if (kind, seg) not in head_box:
            emit_fhead(seg, xbuf, kind)
        t_xn, t_wallbf, tbo = head_box.pop((kind, seg))

        wpe(dve_s, t_wallbf)
        wpe(d_c6, t_c6)
        for dc in range(DC):
            bb = util_bank()
            bb.acquire()
            inst = PE.matmul(bb.t[:, 0:ROWS], b3_sb[0:NE, dc * 128:(dc + 1) * 128],
                             wall_bf[par][:], start=True, stop=True)
            tb3 = bb.done(inst)
            wallbf_free_pe[0] = tb3
            wdve(pe_s, tb3)
            inst = DVE.tensor_tensor(xbuf[:, dc * ROWS:(dc + 1) * ROWS],
                                     xbuf[:, dc * ROWS:(dc + 1) * ROWS],
                                     bb.t[:, 0:ROWS], ALU.add)
            bb.reader(dve_s, inc(inst, dve_s))

        web_tok = [0, 0]
        s_guard = [0, 0]
        t_glu = {}
        t_comb = 0
        last_p2 = 0
        prefetched = {}

        def nxt_pair(e2):
            if e2 in prefetched or e2 >= NE:
                return
            wsync(d_ce[e2], t_weC[e2])
            pf = [gload(w1c[e2][:, :, 0:256], (DC, 256)),
                  gload(w2c[e2][:, :, 0:256], (DC, 256))]
            prefetched[e2] = pf

        for e in range(NE):
            wsync(d_ce[e], t_weC[e])
            g_w1, g_w2 = [None] * 8, [None] * 8
            g_w3 = [None] * DC

            def load_pair(k, e_=e, g_w1=g_w1, g_w2=g_w2):
                if k < 8 and g_w1[k] is None:
                    hc0 = k * 2
                    g_w1[k] = gload(w1c[e_][:, :, hc0 * 128:(hc0 + 2) * 128], (DC, 256))
                    g_w2[k] = gload(w2c[e_][:, :, hc0 * 128:(hc0 + 2) * 128], (DC, 256))

            def load_w3(dc, e_=e, g_w3=g_w3):
                if dc < DC and g_w3[dc] is None:
                    g_w3[dc] = gload(w3c[e_][:, :, dc * 128:(dc + 1) * 128], (HC, 128))

            if e in prefetched:
                g_w1[0], g_w2[0] = prefetched.pop(e)
                load_pair(1)
            else:
                load_pair(0)
                load_pair(1)
            if e < 8:
                wp = e % 2
                wact(d_bco[par], tbo)
                wact(dve_s, web_used[wp])
                inst = ACT.dma_start(
                    out=w_eb[wp][:],
                    in_=wrow_d[par][e:e + 1, :].partition_broadcast(128))
                web_tok[wp] = inc(inst, d_web[wp], 16)

            for hc in range(HC):
                if hc % 2 == 0:
                    if hc // 2 + 2 < 8:
                        load_pair(hc // 2 + 2)
                    elif hc // 2 + 2 == 8:
                        load_w3(0)
                        load_w3(1)
                slot1, slot2 = g_w1[hc // 2], g_w2[hc // 2]
                p1b, p2b = banks[hc % 2], banks[2 + hc % 2]
                p1b.acquire()
                p2b.acquire()
                guse(slot1)
                guse(slot2)
                wpe(dve_s, t_xn)
                off = (hc % 2) * 128
                for dc in range(DC):
                    inst = PE.matmul(
                        p1b.t[:, 0:ROWS],
                        gpool[slot1][:, dc * 256 + off:dc * 256 + off + 128],
                        xnb[:, dc * ROWS:(dc + 1) * ROWS],
                        start=(dc == 0), stop=(dc == DC - 1))
                tp1 = p1b.done(inst)
                for dc in range(DC):
                    inst = PE.matmul(
                        p2b.t[:, 0:ROWS],
                        gpool[slot2][:, dc * 256 + off:dc * 256 + off + 128],
                        xnb[:, dc * ROWS:(dc + 1) * ROWS],
                        start=(dc == 0), stop=(dc == DC - 1))
                tp2 = p2b.done(inst)
                last_p2 = tp2
                if hc % 2 == 1:
                    gdone(slot1, pe_s, tp1)
                    gdone(slot2, pe_s, tp2)
                sp = hc % 2
                wact(pe_s, tp1)
                wact(dve_s, s_guard[sp])
                b1col = b12_sb[:, (e * 2) * HC + hc:(e * 2) * HC + hc + 1]
                tsi = inc(ACT.activation(s_str[sp][:], p1b.t[:, 0:ROWS], AF.Silu,
                                         bias=b1col), act_s)
                p1b.reader(act_s, tsi)
                wdve(pe_s, tp2)
                wdve(act_s, tsi)
                b2col = b12_sb[:, (e * 2 + 1) * HC + hc:(e * 2 + 1) * HC + hc + 1]
                inst = DVE.scalar_tensor_tensor(
                    glu[:, hc * ROWS:(hc + 1) * ROWS], p2b.t[:, 0:ROWS], b2col,
                    s_str[sp][:], ALU.add, ALU.mult)
                t_glu[hc] = s_guard[sp] = inc(inst, dve_s)
                p2b.reader(dve_s, t_glu[hc])
            for dc in range(DC):
                load_w3(dc + 2)
                if dc == 4 and e + 1 < NE:
                    nxt_pair(e + 1)
                slot3 = g_w3[dc]
                wb3 = banks[4 + dc % 2]
                wb3.acquire()
                guse(slot3)
                wpe(dve_s, t_glu[HC - 1])
                for hc in range(HC):
                    inst = PE.matmul(
                        wb3.t[:, 0:ROWS],
                        gpool[slot3][:, hc * 128:(hc + 1) * 128],
                        glu[:, hc * ROWS:(hc + 1) * ROWS],
                        start=(hc == 0), stop=(hc == HC - 1))
                tw3 = wb3.done(inst)
                gdone(slot3, pe_s, tw3)
                wdve(pe_s, tw3)
                if e < 8:
                    wp = e % 2
                    wdve(d_web[wp], web_tok[wp])
                    DVE.tensor_tensor(tmp_str[:], wb3.t[:, 0:ROWS], w_eb[wp][:], ALU.mult)
                    inst = DVE.tensor_tensor(
                        xbuf[:, dc * ROWS:(dc + 1) * ROWS],
                        xbuf[:, dc * ROWS:(dc + 1) * ROWS], tmp_str[:], ALU.add)
                    t_comb = inc(inst, dve_s)
                    web_used[wp] = t_comb
                else:
                    inst = DVE.scalar_tensor_tensor(
                        xbuf[:, dc * ROWS:(dc + 1) * ROWS], wb3.t[:, 0:ROWS], 0.25,
                        xbuf[:, dc * ROWS:(dc + 1) * ROWS], ALU.mult, ALU.add)
                    t_comb = inc(inst, dve_s)
                wb3.reader(dve_s, t_comb)
            if DRAIN_MODE == "interleaved":
                drain(4)
        wrow_free[par] = max(web_tok)
        xn_free_pe[par] = last_p2
        return t_comb

    def emit_p(j):
        seg = j // 2
        half = j % 2
        t0 = j * TP
        parj = j % 2
        ncols_t = min(130, T - t0)
        xb_main = xf[seg % 2]
        if half == 0:
            groups = [(
                (lambda dc, xb=xb_main, b_=b: xb[:, dc * ROWS + b_ * TS:
                                              dc * ROWS + b_ * TS + ncols_t]),
                b * ncols_t, ncols_t) for b in range(2)]
        else:
            m1 = 128
            m2 = ncols_t - m1
            groups = [(
                (lambda dc, xb=xb_main, b_=b: xb[:, dc * ROWS + b_ * TS + 128:
                                              dc * ROWS + b_ * TS + 256]),
                b * ncols_t, m1) for b in range(2)]
            if m2 > 0:
                xb_next = xf[(seg + 1) % 2]
                groups += [(
                    (lambda dc, xb=xb_next, b_=b: xb[:, dc * ROWS + b_ * TS:
                                                  dc * ROWS + b_ * TS + m2]),
                    b * ncols_t + m1, m2) for b in range(2)]
        ncl = 2 * ncols_t
        rb, nb, tbc = emit_ln(groups, ncl, j % 2)
        wdve(pe_s, tbc)
        wdve(pe_s, xn2_free_pe[0])
        inst = None
        for (gslab, coff, cols) in groups:
            for dc in range(DC):
                DVE.tensor_tensor(tmp_str[:, 0:cols], gslab(dc),
                                  rb.t[:, coff:coff + cols], ALU.mult)
                inst = DVE.tensor_tensor(
                    xn2[:, dc * 260 + coff:dc * 260 + coff + cols],
                    tmp_str[:, 0:cols], nb.t[:, coff:coff + cols], ALU.add)
        t_xn2 = inc(inst, dve_s)
        rb.reader(dve_s, t_xn2)
        nb.reader(dve_s, t_xn2)

        wsync(d_c6, t_c6)
        g_win = [None] * 8

        def load_win(k, g_win=g_win):
            if k < 8 and g_win[k] is None:
                g_win[k] = gload(winc[:, :, (k * 2) * 128:(k * 2 + 2) * 128], (DC, 256))

        load_win(0)
        load_win(1)
        t_conv = {}
        last_xz_pe = 0
        for hcz in range(HC):
            if hcz % 2 == 0:
                load_win(hcz // 2 + 2)
            slotw = g_win[hcz // 2]
            zb = util_bank()
            zb.acquire()
            guse(slotw)
            wpe(dve_s, t_xn2)
            offw = (hcz % 2) * 128
            for dc in range(DC):
                inst = PE.matmul(
                    zb.t[:, 0:ncl],
                    gpool[slotw][:, dc * 256 + offw:dc * 256 + offw + 128],
                    xn2[:, dc * 260:dc * 260 + ncl],
                    start=(dc == 0), stop=(dc == DC - 1))
            tzz = zb.done(inst)
            last_xz_pe = tzz
            if hcz % 2 == 1:
                gdone(slotw, pe_s, tzz)
            rp = hcz % 2
            wdve(pe_s, tzz)
            wdve(dve_s, t_conv.get(hcz - 2, 0))
            for b in range(2):
                DVE.tensor_copy(raw[rp][:, b * 131 + 1:b * 131 + 1 + ncols_t],
                                zb.t[:, b * ncols_t:(b + 1) * ncols_t])
                if j == 0:
                    DVE.memset(raw[rp][:, b * 131:b * 131 + 1], 0.0)
                else:
                    DVE.tensor_copy(raw[rp][:, b * 131:b * 131 + 1],
                                    carry[:, hcz * 2 + b:hcz * 2 + b + 1])
                if ncols_t < 130:
                    DVE.memset(raw[rp][:, b * 131 + 1 + ncols_t:(b + 1) * 131], 0.0)
            tev = inc(DVE.tensor_copy(
                carry.ap().rearrange("p (h b2) -> p h b2", b2=2)[:, hcz, :],
                raw[rp].ap().rearrange("p (b c) -> p b c", c=131)[:, :, 128]), dve_s)
            zb.reader(dve_s, tev)
            xzk = 0 if hcz < 8 else 1
            dcz = hcz % 8
            for b in range(2):
                cp = b
                kbase = (xzk * DC + dcz) * 4
                wdve(act_s, max(ca_guard))
                DVE.tensor_scalar(ca[cp][:], raw[rp][:, b * 131:b * 131 + TP],
                                  kk_sb[:, kbase:kbase + 1], None, ALU.mult)
                for tap in range(1, 4):
                    inst = DVE.scalar_tensor_tensor(
                        ca[cp][:], raw[rp][:, b * 131 + tap:b * 131 + tap + TP],
                        kk_sb[:, kbase + tap:kbase + tap + 1], ca[cp][:],
                        ALU.mult, ALU.add)
                t_cv = inc(inst, dve_s)
                wact(dve_s, t_cv)
                if xzk == 0:
                    wact(dve_s, xs_free_dve[parj])
                    inst = ACT.activation(
                        xs[parj][:, dcz * PR + b * TP:dcz * PR + (b + 1) * TP],
                        ca[cp][:], AF.Silu)
                else:
                    wact(pe_s, z_free_pe[0])
                    inst = ACT.activation(
                        z_bf[:, dcz * ROWS + b * TS + half * TP:
                             dcz * ROWS + b * TS + (half + 1) * TP],
                        ca[cp][:], AF.Silu)
                t_conv[hcz] = ca_guard[cp] = inc(inst, act_s)
        xn2_free_pe[0] = last_xz_pe
        xs_act_tok[parj] = t_conv[7]
        z_act_tok[j] = t_conv[15]

        db = util_bank()
        db.acquire()
        wpe(d_c6, t_c6)
        wdve(act_s, t_conv[7])
        tdb = 0
        for dc in range(DC):
            xp = dc % 2
            wdve(pe_s, max(xsb_guard))
            tcast = inc(DVE.tensor_copy(xsb[xp][:], xs[parj][:, dc * PR:(dc + 1) * PR]),
                        dve_s)
            wpe(dve_s, tcast)
            inst = PE.matmul(db.t[0:96, 0:PR], wxp_sb[:, dc * 96:(dc + 1) * 96],
                             xsb[xp][:], start=(dc == 0), stop=(dc == DC - 1))
            xsb_guard[xp] = tdb = inc(inst, pe_s)
        wact(pe_s, tdb)
        wact(pe_s, dtlow_free_pe[parj])
        ACT.activation(dtlow[parj][:], db.t[0:DTR, 0:PR], AF.Copy)
        wact(d_bco[0], bc_out[0])
        wact(d_bco[1], bc_out[1])
        tb2 = inc(ACT.activation(bcst[parj][:], db.t[64:96, 0:PR], AF.Copy), act_s)
        dtlow_act[parj] = tb2
        db.reader(act_s, tb2)
        bc_out[parj] = inc(ACT.dma_start(out=bc_d[parj], in_=bcst[parj][:]), d_bco[parj], 16)

    def emit_sc(j):
        seg = j // 2
        half = j % 2
        parj = j % 2

        def mk_chunk(b, dc):
            def run():
                dmb = util_bank()
                dmb.acquire()
                wpe(act_s, dtlow_act[parj])
                wpe(d_c6, t_c6)
                inst = PE.matmul(dmb.t[:, 0:PR], wdt_sb[:, dc * 128:(dc + 1) * 128],
                                 dtlow[parj][:], start=True, stop=True)
                tdm = dmb.done(inst)
                dtlow_free_pe[parj] = tdm
                wact(pe_s, tdm)
                wact(dve_s, sp_free[0])
                bcol = bdt_sb[:, dc:dc + 1]
                sl = dmb.t[:, b * TP:(b + 1) * TP]
                ACT.activation(sp_r[:], sl, AF.Relu, bias=bcol)
                tso = inc(ACT.activation(sp_a[:], sl, AF.Abs, bias=bcol), act_s)
                dmb.reader(act_s, tso)
                ACT.activation(sp_a[:], sp_a[:], AF.Exp, scale=-1.0)
                t_sp = inc(ACT.activation(sp_a[:], sp_a[:], AF.Ln, bias=onec[:]), act_s)
                if dc == 0:
                    wact(dve_s, BC_B_users[0])
                    wact(d_bco[parj], bc_out[parj])
                    srcB = bc_d[parj][0:16, b * TP:(b + 1) * TP].unsqueeze(0)
                    inc(ACT.dma_start(
                        out=B_sb.ap().rearrange("p (n c) -> p n c", c=TP),
                        in_=srcB.partition_broadcast(128)), d_bcl, 16)
                    wact(gp_s, BC_C_users[0])
                    srcC = bc_d[parj][16:32, b * TP:(b + 1) * TP].unsqueeze(0)
                    tC = inc(ACT.dma_start(
                        out=C_sb.ap().rearrange("p (n c) -> p n c", c=TP),
                        in_=srcC.partition_broadcast(128)), d_bcl, 16)
                    run.tBC = tC
                wdve(act_s, t_sp)
                DVE.tensor_tensor(delta_s[:], sp_r[:], sp_a[:], ALU.add)
                wdve(act_s, xs_act_tok[parj])
                slx = xs[parj][:, dc * PR + b * TP:dc * PR + (b + 1) * TP]
                tdu = inc(DVE.tensor_tensor(du_s[:], delta_s[:], slx, ALU.mult), dve_s)
                sp_free[0] = tdu
                wdve(d_bcl, getattr(run, "tBC", 0))
                wdve(gp_s, hall_gp[0])
                t_scan = 0
                for n in range(NST):
                    ab = a_pool[n % NA]
                    acol = A_sb[:, dc * NST + n:dc * NST + n + 1]
                    ta = inc(DVE.scalar_tensor_tensor(
                        ab[:], delta_s[:], acol, delta_s[:], ALU.mult, ALU.bypass), dve_s)
                    wact(dve_s, ta)
                    tae = inc(ACT.activation(ab[:], ab[:], AF.Exp), act_s)
                    dbb = dbu_b[n % 2]
                    DVE.tensor_tensor(dbb[:], du_s[:], B_sb[:, n * TP:(n + 1) * TP],
                                      ALU.mult)
                    wdve(act_s, tae)
                    hini = h_fin[:, dc * NST * 2 + n * 2 + b:
                                 dc * NST * 2 + n * 2 + b + 1]
                    inst = DVE.tensor_tensor_scan(
                        h_all[:, n * TP:(n + 1) * TP], ab[:], dbb[:], hini,
                        ALU.mult, ALU.add)
                    t_scan = inc(inst, dve_s)
                tfin = inc(DVE.tensor_copy(
                    h_fin.ap().rearrange("p (d2 n b2) -> p d2 n b2", n=NST, b2=2)[:, dc, :, b],
                    h_all.ap().rearrange("p (n c) -> p n c", c=TP)[:, :, TP - 1]), dve_s)
                BC_B_users[0] = t_scan
                wgp(dve_s, tfin)
                wgp(d_bcl, getattr(run, "tBC", 0))
                GP.tensor_tensor(h_all[:], h_all[:], C_sb[:], ALU.mult)
                v = h_all.ap().rearrange("p (n c) -> p n c", c=TP)
                GP.tensor_tensor(v[:, 0:8, :], v[:, 0:8, :], v[:, 8:16, :], ALU.add)
                GP.tensor_tensor(v[:, 0:4, :], v[:, 0:4, :], v[:, 4:8, :], ALU.add)
                GP.tensor_tensor(v[:, 0:2, :], v[:, 0:2, :], v[:, 2:4, :], ALU.add)
                tg = GP.tensor_tensor(v[:, 0:1, :], v[:, 0:1, :], v[:, 1:2, :], ALU.add)
                wdve(gp_s, tg)
                wdve(pe_s, y_free_pe[0])
                ydst = y_bf[:, dc * ROWS + b * TS + half * TP:
                            dc * ROWS + b * TS + (half + 1) * TP]
                ty = DVE.scalar_tensor_tensor(
                    ydst, slx, dp_sb[:, dc:dc + 1], h_all[:, 0:TP],
                    ALU.mult, ALU.add)
                hall_gp[0] = tg
                BC_C_users[0] = tg
                y_dve_tok[j] = ty
                xs_free_dve[parj] = ty
                if b == 1 and dc == DC - 1:
                    sc_done[j] = True
            return run

        for b in range(2):
            for dc in range(DC):
                ch = mk_chunk(b, dc)
                if DRAIN_MODE == "immediate":
                    ch()
                else:
                    pending.append(ch)

    def emit_e(i):
        if not (sc_done[2 * i] and sc_done[2 * i + 1]):
            drain(10 ** 9)
        assert sc_done[2 * i] and sc_done[2 * i + 1], f"SC chunks for E{i} not drained"
        wact(d_sp, spill_tok[i])
        wact(d_out, ebuf_free[0])
        wact(d_tap, d_tap.count)
        reload_tok[i] = inc(ACT.dma_start(out=xe[:], in_=spill_d[i]), d_rl, 16)
        wsync(d_c6, t_c6)
        g_wo = [None] * DC

        def load_wo(dc, g_wo=g_wo):
            if dc < DC and g_wo[dc] is None:
                g_wo[dc] = gload(woutc[:, :, dc * 128:(dc + 1) * 128], (HC, 128))

        load_wo(0)
        load_wo(1)
        wpe(dve_s, y_dve_tok[2 * i + 1])
        wpe(act_s, z_act_tok[2 * i + 1])
        wdve(d_rl, reload_tok[i])
        two = 0
        for dco in range(DC):
            load_wo(dco + 2)
            wb = banks[4 + dco % 2]
            wb.acquire()
            slotw = g_wo[dco]
            guse(slotw)
            for kc in range(HC):
                rhs = y_bf if kc < 8 else z_bf
                dcs = kc % 8
                inst = PE.matmul(
                    wb.t[:, 0:ROWS],
                    gpool[slotw][:, kc * 128:(kc + 1) * 128],
                    rhs[:, dcs * ROWS:(dcs + 1) * ROWS],
                    start=(kc == 0), stop=(kc == HC - 1))
            two = wb.done(inst)
            gdone(slotw, pe_s, two)
            wdve(pe_s, two)
            inst = DVE.scalar_tensor_tensor(
                xe[:, dco * ROWS:(dco + 1) * ROWS], wb.t[:, 0:ROWS],
                bout_sb[:, dco:dco + 1],
                xe[:, dco * ROWS:(dco + 1) * ROWS], ALU.add, ALU.add)
            wb.reader(dve_s, inc(inst, dve_s))
        y_free_pe[0] = two
        z_free_pe[0] = two
        t_done = emit_ffn(i, xe, "E")
        wact(dve_s, t_done)
        for b in range(2):
            dst = out[b, :, i * TS:(i + 1) * TS].rearrange("(dc p) t -> p dc t", p=128)
            src = xe.ap().rearrange("p (dc b t) -> p dc b t", b=2, t=TS)[:, :, b, :]
            out_tok[i] = inc(ACT.dma_start(out=dst, in_=src), d_out, 16)
        ebuf_free[0] = out_tok[i]

    def emit_x(i):
        wact(dve_s, xseg_done[i])
        spill_tok[i] = inc(ACT.dma_start(out=spill_d[i], in_=xf[i % 2][:]), d_sp, 16)
        wact(d_sp, spill_tok[i])
        if i + 2 < NSEG:
            load_x(i + 2)

    def emit_a(i):
        wdve(d_in[i % 2], x_load_tok[i])
        emit_ffn(i, xf[i % 2], "A")

    sched = _sched()
    if STAGE_LIMIT is not None:
        sched = sched[:STAGE_LIMIT]
    for stg in sched:
        if stg.startswith("SC"):
            emit_sc(int(stg[2:]))
        elif stg[0] == "H":
            k = int(stg[1:])
            wdve(d_in[k % 2], x_load_tok[k])
            emit_fhead(k, xf[k % 2], "A")
        elif stg[0] == "A":
            emit_a(int(stg[1:]))
        elif stg[0] == "P":
            j = int(stg[1:])
            emit_p(j)
            if j % 2 == 1:
                xseg_done[j // 2] = dve_s.count
        elif stg[0] == "E":
            emit_e(int(stg[1:]))
        elif stg[0] == "X":
            emit_x(int(stg[1:]))
        if stg in taps:
            drain(10 ** 9)
            src = xe if stg[0] == "E" else xf[int(stg[1:]) % 2]
            wact(dve_s, dve_s.count)
            wact(gp_s, gp_s.count)
            inc(ACT.dma_start(out=taps[stg], in_=src[:]), d_tap, 16)

    drain(10 ** 9)
    for name in DEBUG_DUMP:
        t, shp, dt = sreg[name]
        dd = nc.dram_tensor("dump_" + name, shp, dt, kind="ExternalOutput").ap()
        wact(dve_s, dve_s.count)
        wact(act_s, act_s.count)
        wact(gp_s, gp_s.count)
        wact(pe_s, pe_s.count)
        inc(ACT.dma_start(out=dd, in_=t[:]), d_tap, 16)
    for s_ in sems:
        if s_.name.startswith("d_"):
            wsync(s_, s_.count)

    st.close()
    for s_ in sems:
        s_._cm.__exit__(None, None, None)
    return nc


def _host_prep(inputs):
    x = np.ascontiguousarray(inputs["L0_attn_x"])
    f32 = np.float32
    w1 = np.concatenate([inputs["L0_ffn_rw1"], inputs["L0_ffn_sw1"]], 0).reshape(NE, DC, 128, H)
    w2 = np.concatenate([inputs["L0_ffn_rw2"], inputs["L0_ffn_sw2"]], 0).reshape(NE, DC, 128, H)
    w3 = np.concatenate([inputs["L0_ffn_rw3"], inputs["L0_ffn_sw3"]], 0).reshape(NE, HC, 128, D)
    win = np.asarray(inputs["L0_attn_W_in"]).reshape(DC, 128, H)
    wout = np.asarray(inputs["L0_attn_W_out"]).reshape(HC, 128, D)
    wxp = np.asarray(inputs["L0_attn_W_xp"]).reshape(DC, 128, 96)
    wdt = np.ascontiguousarray(inputs["L0_attn_W_dt"])
    gw = np.asarray(inputs["L0_ffn_gate_w"]).reshape(DC, 128, 8)
    b1 = np.concatenate([inputs["L0_ffn_rb1"], inputs["L0_ffn_sb1"]], 0)
    b2 = np.concatenate([inputs["L0_ffn_rb2"], inputs["L0_ffn_sb2"]], 0)
    b12 = np.stack([b1.reshape(NE, HC, 128), b2.reshape(NE, HC, 128)], 1)
    b12 = np.ascontiguousarray(b12.transpose(3, 0, 1, 2)).reshape(128, NE * 2 * HC)
    b3 = np.concatenate([inputs["L0_ffn_rb3"], inputs["L0_ffn_sb3"]], 0)
    bdt = np.ascontiguousarray(np.asarray(inputs["L0_attn_b_dt"]).reshape(DC, 128).T)
    alog = np.ascontiguousarray(
        np.asarray(inputs["L0_attn_A_log"]).reshape(DC, 128, NST).transpose(1, 0, 2)
    ).reshape(128, DC * NST)
    dp = np.ascontiguousarray(np.asarray(inputs["L0_attn_D_param"]).reshape(DC, 128).T)
    bout = np.ascontiguousarray(np.asarray(inputs["L0_attn_b_out"]).reshape(DC, 128).T)
    kx = np.asarray(inputs["L0_attn_K_x"])[:, 0, :].reshape(4, DC, 128)
    kz = np.asarray(inputs["L0_attn_K_z"])[:, 0, :].reshape(4, DC, 128)
    kk = np.stack([kx, kz], 0)
    kk = np.ascontiguousarray(kk.transpose(3, 0, 2, 1)).reshape(128, 2 * DC * 4)
    gb = np.asarray(inputs["L0_ffn_gate_b"]).reshape(8, 1)
    eye = np.eye(128, dtype=f32)

    shared = dict(w1f=w1, w2f=w2, w3f=w3, winf=win, woutf=wout, wxpf=wxp,
                  wdtf=wdt, gwf=gw, b12=b12, b3=b3, bdt=bdt, alog=alog,
                  dp=dp, bout=bout, kk=kk, gb=gb, eye=eye)
    shared = {k: np.ascontiguousarray(np.asarray(v, dtype=f32)) for k, v in shared.items()}

    in_maps = []
    for c in range(8):
        m = dict(shared)
        m["x"] = np.ascontiguousarray(
            np.asarray(x[2 * c:2 * c + 2], dtype=f32).transpose(0, 2, 1))
        in_maps.append(m)
    return in_maps


def kernel(**inputs):
    in_maps = _host_prep(inputs)
    nc = build()
    res = run_bass_kernel_spmd(nc, in_maps, core_ids=list(range(8)))
    outs = [res.results[c]["out"].transpose(0, 2, 1) for c in range(8)]
    return np.ascontiguousarray(np.concatenate(outs, 0)).astype(np.float32)


# version 05 (final)
# speedup vs baseline: 1.0275x; 1.0275x over previous
import numpy as np
import ml_dtypes
import concourse.bass as bass
import concourse.mybir as mybir
from concourse.bass_utils import run_bass_kernel_spmd
from contextlib import ExitStack
from collections import deque

F32 = mybir.dt.float32
BF16 = mybir.dt.bfloat16
AF = mybir.ActivationFunctionType
ALU = mybir.AluOpType
AX = mybir.AxisListType

T = 2048
D = 1024
DC = 8
H = 2048
HC = 16
NE = 10
NST = 16
DTR = 64
NSEG = 8
TS = 256
ROWS = 512
NP = 16
TP = 128
PR = 256
EPS = 1e-5

DEBUG_TAPS = ()
DEBUG_DUMP = ()
STAGE_LIMIT = None
DRAIN_MODE = "interleaved"


def _sched():
    s = ["A0", "A1"]
    for i in range(0, 6):
        s += [f"P{2*i}", f"SC{2*i}", f"P{2*i+1}", f"SC{2*i+1}", f"X{i}", f"H{i+2}"]
        s += [f"A{i+2}", f"E{i}"]
    s += ["P12", "SC12", "P13", "SC13", "X6", "E6"]
    s += ["P14", "SC14", "P15", "SC15", "X7", "E7"]
    return s


class Sem:
    def __init__(self, nc, name):
        self._cm = nc.semaphore(name)
        self.h = self._cm.__enter__()
        self.count = 0
        self.name = name


def build():
    nc = bass.Bass("TRN2", target_bir_lowering=False, debug=False)
    last_wait = {}

    def _wait(eng, engname, sem, val):
        if not val or val <= 0:
            return
        key = (engname, sem.name)
        if last_wait.get(key, 0) >= val:
            return
        eng.wait_ge(sem.h, val)
        last_wait[key] = val

    def din(name, shape):
        return nc.dram_tensor(name, list(shape), F32, kind="ExternalInput").ap()

    x_in = din("x", (2, D, T))
    w1f = din("w1f", (NE, DC, 128, H))
    w2f = din("w2f", (NE, DC, 128, H))
    w3f = din("w3f", (NE, HC, 128, D))
    winf = din("winf", (DC, 128, H))
    woutf = din("woutf", (HC, 128, D))
    wxpf = din("wxpf", (DC, 128, 96))
    wdtf = din("wdtf", (DTR, D))
    gwf = din("gwf", (DC, 128, 8))
    b12_in = din("b12", (128, NE * 2 * HC))
    b3_in = din("b3", (NE, D))
    bdt_in = din("bdt", (128, DC))
    alog_in = din("alog", (128, DC * NST))
    dp_in = din("dp", (128, DC))
    bout_in = din("bout", (128, DC))
    kk_in = din("kk", (128, 2 * DC * 4))
    gb_in = din("gb", (8, 1))
    eye_in = din("eye", (128, 128))

    out = nc.dram_tensor("out", [2, D, T], F32, kind="ExternalOutput").ap()

    w1c = nc.dram_tensor("w1c", [NE, DC, 128, H], BF16, kind="Internal").ap()
    w2c = nc.dram_tensor("w2c", [NE, DC, 128, H], BF16, kind="Internal").ap()
    w3c = nc.dram_tensor("w3c", [NE, HC, 128, D], BF16, kind="Internal").ap()
    winc = nc.dram_tensor("winc", [DC, 128, H], BF16, kind="Internal").ap()
    woutc = nc.dram_tensor("woutc", [HC, 128, D], BF16, kind="Internal").ap()
    spill_d = nc.dram_tensor("spill", [NSEG, 128, DC * ROWS], F32, kind="Internal").ap()
    wrow_d = nc.dram_tensor("wrow", [2, 8, ROWS], F32, kind="Internal").ap()
    bc_d = nc.dram_tensor("bcd", [2, 32, 2 * TP], F32, kind="Internal").ap()

    taps = {}
    for tname in DEBUG_TAPS:
        taps[tname] = nc.dram_tensor(
            "tap_" + tname, [128, DC * ROWS], F32, kind="ExternalOutput").ap()

    st = ExitStack()

    sreg = {}

    def sb(name, shape, dt):
        t = st.enter_context(nc.sbuf_tensor(name, list(shape), dt))
        sreg[name] = (t, list(shape), dt)
        return t

    xf = [sb(f"xf{i}", (128, DC * ROWS), F32) for i in range(2)]
    xe = sb("xe", (128, DC * ROWS), F32)
    xn = [sb(f"xn{i}", (128, DC * ROWS), BF16) for i in range(2)]
    xn2 = sb("xn2", (128, DC * 2 * 130), BF16)
    glu = sb("glu", (128, HC * ROWS), BF16)
    s_str = [sb(f"sstr{i}", (128, ROWS), BF16) for i in range(2)]
    tmp_str = sb("tmpstr", (128, ROWS), F32)
    w_eb = [sb(f"web{i}", (128, ROWS), F32) for i in range(2)]
    wall = [sb(f"wall{i}", (NE, ROWS), F32) for i in range(2)]
    wall_bf = [sb("wallbf0", (NE, ROWS), BF16)] * 2
    NGB = 6
    gpool = [sb(f"gwb{i}", (128, 2048), BF16) for i in range(NGB)]
    wxp_sb = sb("wxp", (128, DC * 96), BF16)
    wdt_sb = sb("wdt", (DTR, D), BF16)
    gw_sb = sb("gwt", (128, DC * 8), BF16)
    xc = [sb(f"xcc{i}", (128, ROWS), BF16) for i in range(2)]
    x2c = [sb(f"x2c{i}", (128, ROWS), BF16) for i in range(2)]
    rows_sb = sb("rows", (1, 1552), F32)
    lgt = sb("lgt", (8, ROWS), F32)
    g_eo = sb("geo", (128, 32), F32)
    srt4 = sb("srt4", (128, 32), F32)
    gmsk = sb("gmsk", (128, 32), F32)
    raw = [sb("raw0", (128, 2 * 131), F32)] * 2
    ca = [sb("cacc0", (128, TP), F32)] * 2
    carry = sb("carry", (128, HC * 2), F32)
    xs = [sb(f"xs{i}", (128, DC * PR), F32) for i in range(2)]
    xsb = [sb("xsb0", (128, PR), BF16)] * 2
    z_bf = sb("zbf", (128, DC * ROWS), BF16)
    y_bf = sb("ybf", (128, DC * ROWS), BF16)
    dtlow = [sb(f"dtlow{i}", (DTR, PR), BF16) for i in range(2)]
    bcst = [sb("bcst0", (32, 2 * TP), F32)] * 2
    delta_s = sb("delta", (128, TP), F32)
    du_s = sb("du", (128, TP), F32)
    NA = 2
    a_pool = [sb(f"apool{i}", (128, TP), F32) for i in range(NA)]
    dbu_b = [sb(f"dbu{i}", (128, TP), F32) for i in range(2)]
    B_sb = sb("Bsb", (128, NST * TP), F32)
    C_sb = sb("Csb", (128, NST * TP), F32)
    h_all = sb("hall", (128, NST * TP), F32)
    h_fin = sb("hfin", (128, DC * NST * 2), F32)
    sp_r = sb("spr", (128, TP), F32)
    sp_a = sb("spa", (128, TP), F32)
    b12_sb = sb("b12s", (128, NE * 2 * HC), F32)
    b3_sb = sb("b3s", (NE, D), BF16)
    A_sb = sb("Asb", (128, DC * NST), F32)
    bdt_sb = sb("bdts", (128, DC), F32)
    dp_sb = sb("dps", (128, DC), F32)
    bout_sb = sb("bouts", (128, DC), F32)
    kk_sb = sb("kks", (128, 2 * DC * 4), F32)
    gb_sb = sb("gbs", (8, 1), F32)
    eye_sb = sb("eyes", (128, 128), F32)
    ones_bf = sb("onesbf", (128, 1), BF16)
    ones_f = sb("onesf", (1, 128), F32)
    onec = sb("onec", (128, 1), F32)
    epsr = sb("epsr", (1, 1), F32)

    pb = [st.enter_context(nc.psum_tensor(f"pb{i}", [128, 512], F32)) for i in range(8)]

    d_g = [Sem(nc, f"d_g{i}") for i in range(NGB)]
    d_sm = Sem(nc, "d_sm")
    d_c6 = Sem(nc, "d_c6")
    d_ce = [Sem(nc, f"d_ce{e}") for e in range(NE)]
    d_in = [Sem(nc, "d_in0"), Sem(nc, "d_in1")]
    d_sp = Sem(nc, "d_sp")
    d_rl = Sem(nc, "d_rl")
    d_out = Sem(nc, "d_out")
    d_web = [Sem(nc, "d_web0"), Sem(nc, "d_web1")]
    d_bco = [Sem(nc, "d_bco0"), Sem(nc, "d_bco1")]
    d_bcl = Sem(nc, "d_bcl")
    d_tap = Sem(nc, "d_tap")
    pe_s = Sem(nc, "pe_s")
    dve_s = Sem(nc, "dve_s")
    act_s = Sem(nc, "act_s")
    gp_s = Sem(nc, "gp_s")
    sems = (d_g + [d_sm, d_c6] + d_ce + d_in +
            [d_sp, d_rl, d_out] + d_web + d_bco + [d_bcl, d_tap, pe_s, dve_s, act_s, gp_s])

    PE, SYNC = nc.tensor, nc.sync

    def _ap_span(x):
        try:
            t = x.tensor
            name = t.name
            off = x.offset
            hi = off + 1
            for (step, cnt) in list(x.ap)[1:]:
                hi += abs(step) * (cnt - 1)
            return (name, off, hi)
        except Exception:
            try:
                return (x.tensor.name, 0, 1 << 60)
            except Exception:
                return None

    class EngW:

        def __init__(self, eng, name, sem):
            self._eng = eng
            self._name = name
            self._sem = sem
            self._acc = {}

        def wait_ge(self, *a, **k):
            return self._eng.wait_ge(*a, **k)

        def _conflict(self, spans, writing):
            need = 0
            waited = last_wait.get((self._name, self._sem.name), 0)
            for (name, lo, hi) in spans:
                lst = self._acc.get(name)
                if not lst:
                    continue
                keep = []
                for (tok, l2, h2, w2) in lst:
                    if tok <= waited:
                        continue
                    keep.append((tok, l2, h2, w2))
                    if (w2 or writing) and l2 < hi and lo < h2:
                        if tok > need:
                            need = tok
                self._acc[name] = keep
            return need

        def _record(self, spans, is_write):
            tok = self._sem.count
            for (name, lo, hi) in spans:
                self._acc.setdefault(name, []).append((tok, lo, hi, is_write))

        def __getattr__(self, op):
            real = getattr(self._eng, op)

            def f(*a, **k):
                aps = [x for x in list(a) + list(k.values())
                       if hasattr(x, "tensor") or hasattr(x, "ap")]
                spans = [sp for sp in (_ap_span(x) for x in aps) if sp]
                if op in ("dma_start", "indirect_dma_start"):
                    need = self._conflict(spans, True)
                    _wait(self._eng, self._name, self._sem, need)
                    return real(*a, **k)
                wsp = spans[:1]
                rsp = spans[1:]
                need = max(self._conflict(rsp, False), self._conflict(wsp, True))
                _wait(self._eng, self._name, self._sem, need)
                inst = real(*a, **k)
                inst.then_inc(self._sem.h, 1)
                self._sem.count += 1
                self._record(wsp, True)
                self._record(rsp, False)
                return self._sem.count
            return f

    DVE = EngW(nc.vector, "dve", dve_s)
    ACT = EngW(nc.scalar, "act", act_s)
    GP = EngW(nc.gpsimd, "gp", gp_s)

    def inc(inst, sem, n=1):
        if isinstance(inst, int):
            return inst
        inst.then_inc(sem.h, n)
        sem.count += n
        return sem.count

    def wpe(sem, val):
        _wait(PE, "pe", sem, val)

    def wdve(sem, val):
        _wait(nc.vector, "dve", sem, val)

    def wact(sem, val):
        _wait(nc.scalar, "act", sem, val)

    def wsync(sem, val):
        _wait(SYNC, "sync", sem, val)

    def wgp(sem, val):
        _wait(nc.gpsimd, "gp", sem, val)

    class Bank:
        def __init__(self, i):
            self.i = i
            self.t = pb[i]
            self.rtoks = []

        def acquire(self):
            for (sem, val) in self.rtoks:
                wpe(sem, val)
            self.rtoks = []

        def done(self, inst):
            return inc(inst, pe_s)

        def reader(self, sem, val):
            self.rtoks.append((sem, val))

    banks = [Bank(i) for i in range(8)]
    util_rr = [0]

    def util_bank():
        b = banks[6 + util_rr[0] % 2]
        util_rr[0] += 1
        return b

    gp_rtoks = [[] for _ in range(NGB)]
    gp_ltok = [0] * NGB
    gp_i = [0]

    gp_resv = [False] * NGB

    def gload(in_ap, shape3, q=0):
        for _ in range(NGB):
            slot = gp_i[0] % NGB
            gp_i[0] += 1
            if not gp_resv[slot]:
                break
        else:
            raise RuntimeError(f"granule pool exhausted resv={gp_resv} i={gp_i[0]}")
        gp_resv[slot] = True
        for (sem, val) in gp_rtoks[slot]:
            if q == 0:
                wsync(sem, val)
            else:
                wact(sem, val)
        gp_rtoks[slot] = []
        a_, b_ = shape3
        dst = gpool[slot].ap()[:, 0:a_ * b_].rearrange("p (a b) -> p a b", b=b_)
        src = in_ap.rearrange("a p b -> p a b")
        if q == 0:
            inst = SYNC.dma_start(out=dst, in_=src)
        else:
            inst = ACT.dma_start(out=dst, in_=src)
        gp_ltok[slot] = inc(inst, d_g[slot], 16)
        return slot

    def guse(slot):
        wpe(d_g[slot], gp_ltok[slot])

    def gdone(slot, sem, val):
        if val:
            gp_rtoks[slot].append((sem, val))
            gp_resv[slot] = False

    def sm_dma(out_ap, in_ap):
        return inc(ACT.dma_start(out=out_ap, in_=in_ap), d_sm, 16)

    sm_dma(b12_sb[:], b12_in)
    sm_dma(A_sb[:], alog_in)
    sm_dma(bdt_sb[:], bdt_in)
    sm_dma(dp_sb[:], dp_in)
    sm_dma(bout_sb[:], bout_in)
    sm_dma(kk_sb[:], kk_in)
    sm_dma(gb_sb[:], gb_in)
    t_small = sm_dma(eye_sb[:], eye_in)

    def cast6(out_ap, in_ap):
        return inc(GP.dma_start(out=out_ap, in_=in_ap), d_c6, 16)

    cast6(wxp_sb.ap().rearrange("p (a b) -> p a b", b=96),
          wxpf.rearrange("a p b -> p a b"))
    cast6(wdt_sb[:], wdtf)
    cast6(gw_sb.ap().rearrange("p (a b) -> p a b", b=8),
          gwf.rearrange("a p b -> p a b"))
    cast6(b3_sb[:], b3_in)
    cast6(winc.rearrange("a p b -> (a p b)"), winf.rearrange("a p b -> (a p b)"))
    t_c6 = cast6(woutc.rearrange("a p b -> (a p b)"), woutf.rearrange("a p b -> (a p b)"))
    t_weC = []
    for e in range(NE):
        inc(GP.dma_start(out=w1c[e].rearrange("a p b -> (a p b)"),
                         in_=w1f[e].rearrange("a p b -> (a p b)")), d_ce[e], 16)
        inc(GP.dma_start(out=w2c[e].rearrange("a p b -> (a p b)"),
                         in_=w2f[e].rearrange("a p b -> (a p b)")), d_ce[e], 16)
        t_weC.append(
            inc(GP.dma_start(out=w3c[e].rearrange("a p b -> (a p b)"),
                             in_=w3f[e].rearrange("a p b -> (a p b)")), d_ce[e], 16))

    wact(d_sm, t_small)
    tA = inc(ACT.activation(A_sb[:], A_sb[:], AF.Exp), act_s)
    wdve(act_s, tA)
    DVE.tensor_scalar_mul(A_sb[:], A_sb[:], -1.0)
    DVE.memset(ones_bf[:], 1.0)
    DVE.memset(ones_f[:], 1.0)
    DVE.memset(h_fin[:], 0.0)
    DVE.memset(carry[:], 0.0)
    DVE.memset(wall[0][:], 0.25)
    DVE.memset(onec[:], 1.0)
    DVE.memset(epsr[:], EPS)
    inc(DVE.memset(wall[1][:], 0.25), dve_s)

    x_load_tok = [0] * NSEG
    xseg_done = [0] * NSEG
    spill_tok = [0] * NSEG
    reload_tok = [0] * NSEG
    out_tok = [0] * NSEG
    y_dve_tok = [0] * NP
    z_act_tok = [0] * NP
    ebuf_free = [0]
    y_free_pe = [0]
    z_free_pe = [0]
    xn_free_pe = [0, 0]
    wallbf_free_pe = [0]
    xn2_free_pe = [0]
    xs_free_dve = [0, 0]
    xs_act_tok = [0, 0]
    dtlow_act = [0, 0]
    dtlow_free_pe = [0, 0]
    bc_out = [0, 0]
    bcst_free = [0]
    wrow_free = [0, 0]
    web_used = [0, 0]
    xc_guard = {}
    BC_B_users = [0]
    BC_C_users = [0]
    hall_gp = [0]
    sp_free = [0]
    xsb_guard = [0, 0]
    ca_guard = [0, 0]
    pending = deque()
    sc_done = [False] * NP

    def load_x(i):
        wact(d_tap, d_tap.count)
        t0 = i * TS
        for b in range(2):
            src = x_in[b, :, t0:t0 + TS].rearrange("(dc p) t -> p dc t", p=128)
            dst = xf[i % 2].ap().rearrange("p (dc b t) -> p dc b t", b=2, t=TS)[:, :, b, :]
            x_load_tok[i] = inc(ACT.dma_start(out=dst, in_=src), d_in[i % 2], 16)

    load_x(0)
    load_x(1)

    def emit_ln(groups, ncols, which):
        s0b = util_bank()
        s1b = util_bank()
        s0b.acquire()
        s1b.acquire()
        first = True
        ngr = len(groups)
        for gi_, (gslab, coff, cols) in enumerate(groups):
            for dc in range(DC):
                pp = (which + dc) % 2
                if xc_guard.get(pp):
                    wdve(pe_s, xc_guard[pp])
                src = gslab(dc)
                DVE.tensor_copy(xc[pp][:, 0:cols], src)
                t2 = inc(DVE.tensor_tensor(x2c[pp][:, 0:cols], src, src, ALU.mult), dve_s)
                wpe(dve_s, t2)
                last = (gi_ == ngr - 1 and dc == DC - 1)
                PE.matmul(s0b.t[0:1, coff:coff + cols], ones_bf[:], xc[pp][:, 0:cols],
                          start=first, stop=last)
                inst = PE.matmul(s1b.t[0:1, coff:coff + cols], ones_bf[:],
                                 x2c[pp][:, 0:cols], start=first, stop=last)
                xc_guard[pp] = inc(inst, pe_s)
                first = False
        tstats = xc_guard[(which + DC - 1) % 2]
        m = rows_sb[0:1, 0:ncols]
        r = rows_sb[0:1, 520:520 + ncols]
        nmr = rows_sb[0:1, 1040:1040 + ncols]
        wdve(pe_s, tstats)
        DVE.tensor_scalar_mul(m, s0b.t[0:1, 0:ncols], 1.0 / D)
        DVE.tensor_scalar_mul(r, s1b.t[0:1, 0:ncols], 1.0 / D)
        DVE.tensor_tensor(nmr, m, m, ALU.mult)
        tv = inc(DVE.tensor_tensor(r, r, nmr, ALU.subtract), dve_s)
        s0b.reader(dve_s, tv)
        s1b.reader(dve_s, tv)
        wact(dve_s, tv)
        ACT.activation(r, r, AF.Ln, bias=epsr[:])
        tr = inc(ACT.activation(r, r, AF.Exp, scale=-0.5), act_s)
        wdve(act_s, tr)
        trows = inc(DVE.scalar_tensor_tensor(nmr, m, -1.0, r, ALU.mult, ALU.mult), dve_s)
        rb = util_bank()
        nb = util_bank()
        rb.acquire()
        nb.acquire()
        wpe(dve_s, trows)
        rb.done(PE.matmul(rb.t[:, 0:ncols], ones_f[:], r, start=True, stop=True))
        tb = nb.done(PE.matmul(nb.t[:, 0:ncols], ones_f[:], nmr, start=True, stop=True))
        return rb, nb, tb

    def emit_gate(par, xn_tok):
        xnb = xn[par]
        lb = util_bank()
        lb.acquire()
        wpe(dve_s, xn_tok)
        wpe(d_c6, t_c6)
        for dc in range(DC):
            inst = PE.matmul(lb.t[0:8, 0:ROWS], gw_sb[:, dc * 8:(dc + 1) * 8],
                             xnb[:, dc * ROWS:(dc + 1) * ROWS],
                             start=(dc == 0), stop=(dc == DC - 1))
        tl = lb.done(inst)
        wdve(pe_s, tl)
        wdve(d_sm, t_small)
        tlg = inc(DVE.tensor_scalar(lgt[:], lb.t[0:8, 0:ROWS], gb_sb[:], None, ALU.add),
                  dve_s)
        lb.reader(dve_s, tlg)
        tb = util_bank()
        tb.acquire()
        wpe(dve_s, tlg)
        wpe(d_sm, t_small)
        for blk in range(4):
            inst = PE.transpose(tb.t[:, blk * 8:(blk + 1) * 8],
                                lgt[0:8, blk * 128:(blk + 1) * 128], eye_sb[0:8, 0:8])
        ttp = tb.done(inst)
        wdve(pe_s, ttp)
        DVE.tensor_copy(
            g_eo.ap().rearrange("p (e k) -> p k e", k=4),
            tb.t[:, 0:32].rearrange("p (k e) -> p k e", e=8))
        tge = inc(DVE.tensor_copy(srt4[:], tb.t[:, 0:32]), dve_s)
        tb.reader(dve_s, tge)
        mx = gmsk[:, 0:4]
        DVE.reduce_max(mx, srt4.ap().rearrange("p (k e) -> p k e", e=8), AX.X)
        v_eo = g_eo.ap().rearrange("p (e k) -> p e k", k=4)
        tsb = inc(DVE.tensor_tensor(
            v_eo, v_eo, mx.unsqueeze(1).broadcast_to([128, 8, 4]), ALU.subtract), dve_s)
        wact(dve_s, tsb)
        tex = inc(ACT.activation(g_eo[:], g_eo[:], AF.Exp), act_s)
        wdve(act_s, tex)
        sc1 = gmsk[:, 8:24].rearrange("p (e k) -> p e k", k=4)
        DVE.tensor_tensor(sc1, v_eo[:, 0:4, :], v_eo[:, 4:8, :], ALU.add)
        sc2 = gmsk[:, 24:32].rearrange("p (e k) -> p e k", k=4)
        DVE.tensor_tensor(sc2, sc1[:, 0:2, :], sc1[:, 2:4, :], ALU.add)
        ssum = gmsk[:, 4:8]
        DVE.tensor_tensor(ssum, gmsk[:, 24:28], gmsk[:, 28:32], ALU.add)
        DVE.reciprocal(ssum, ssum)
        DVE.tensor_tensor(v_eo, v_eo, ssum.unsqueeze(1).broadcast_to([128, 8, 4]), ALU.mult)
        vk = g_eo.ap().rearrange("p (e k) -> p k e", k=4)
        for blk in range(4):
            DVE.max(srt4[:, blk * 8:(blk + 1) * 8], vk[:, blk, :])
        m2 = srt4.ap().rearrange("p (k s) -> p s k", s=8)[:, 1, :]
        DVE.tensor_tensor(gmsk.ap().rearrange("p (e k) -> p e k", k=4),
                          v_eo, m2.unsqueeze(1).broadcast_to([128, 8, 4]), ALU.is_ge)
        tw = inc(DVE.scalar_tensor_tensor(
            g_eo[:], g_eo[:], 0.5, gmsk[:], ALU.mult, ALU.mult), dve_s)
        wb = util_bank()
        wb.acquire()
        wpe(dve_s, tw)
        for blk in range(4):
            inst = PE.transpose(wb.t[0:8, blk * 128:(blk + 1) * 128],
                                vk[:, blk, :], eye_sb[:])
        twb = wb.done(inst)
        wdve(pe_s, twb)
        t_wall = inc(DVE.tensor_copy(wall[par][0:8, :], wb.t[0:8, 0:ROWS]), dve_s)
        wb.reader(dve_s, t_wall)
        wdve(pe_s, wallbf_free_pe[0])
        t_wallbf = inc(DVE.tensor_copy(wall_bf[par][:], wall[par][:]), dve_s)
        wact(dve_s, t_wall)
        wact(d_web[0], wrow_free[par])
        wact(d_web[1], wrow_free[par])
        tbo = inc(ACT.dma_start(out=wrow_d[par], in_=wall[par][0:8, :]), d_bco[par], 16)
        return t_wallbf, tbo

    def drain(k):
        for _ in range(k):
            if not pending:
                return
            pending.popleft()()

    head_box = {}

    def emit_fhead(seg, xbuf, kind):
        par = 0 if kind == "A" else 1
        xnb = xn[par]
        which = (seg + (0 if kind == "A" else 1)) % 2
        groups = [(lambda dc, xb=xbuf: xb[:, dc * ROWS:(dc + 1) * ROWS], 0, ROWS)]
        rb, nb, tbc = emit_ln(groups, ROWS, which)
        wdve(pe_s, tbc)
        wdve(pe_s, xn_free_pe[par])
        for dc in range(DC):
            DVE.tensor_tensor(tmp_str[:], xbuf[:, dc * ROWS:(dc + 1) * ROWS],
                              rb.t[:, 0:ROWS], ALU.mult)
            inst = DVE.tensor_tensor(xnb[:, dc * ROWS:(dc + 1) * ROWS],
                                     tmp_str[:], nb.t[:, 0:ROWS], ALU.add)
        t_xn = inc(inst, dve_s)
        rb.reader(dve_s, t_xn)
        nb.reader(dve_s, t_xn)
        t_wallbf, tbo = emit_gate(par, t_xn)
        head_box[(kind, seg)] = (t_xn, t_wallbf, tbo)

    def emit_ffn(seg, xbuf, kind):
        par = 0 if kind == "A" else 1
        xnb = xn[par]
        if (kind, seg) not in head_box:
            emit_fhead(seg, xbuf, kind)
        t_xn, t_wallbf, tbo = head_box.pop((kind, seg))

        wpe(dve_s, t_wallbf)
        wpe(d_c6, t_c6)
        for dc in range(DC):
            bb = util_bank()
            bb.acquire()
            inst = PE.matmul(bb.t[:, 0:ROWS], b3_sb[0:NE, dc * 128:(dc + 1) * 128],
                             wall_bf[par][:], start=True, stop=True)
            tb3 = bb.done(inst)
            wallbf_free_pe[0] = tb3
            wdve(pe_s, tb3)
            inst = DVE.tensor_tensor(xbuf[:, dc * ROWS:(dc + 1) * ROWS],
                                     xbuf[:, dc * ROWS:(dc + 1) * ROWS],
                                     bb.t[:, 0:ROWS], ALU.add)
            bb.reader(dve_s, inc(inst, dve_s))

        web_tok = [0, 0]
        s_guard = [0, 0]
        t_glu = {}
        t_comb = 0
        last_p2 = 0
        prefetched = {}

        def nxt_pair(e2):
            if e2 in prefetched or e2 >= NE:
                return
            wsync(d_ce[e2], t_weC[e2])
            wact(d_ce[e2], t_weC[e2])
            pf = [gload(w1c[e2][:, :, 0:256], (DC, 256)),
                  gload(w2c[e2][:, :, 0:256], (DC, 256), q=1)]
            prefetched[e2] = pf

        for e in range(NE):
            wsync(d_ce[e], t_weC[e])
            wact(d_ce[e], t_weC[e])
            g_w1, g_w2 = [None] * 8, [None] * 8
            g_w3 = [None] * DC

            def load_pair(k, e_=e, g_w1=g_w1, g_w2=g_w2):
                if k < 8 and g_w1[k] is None:
                    hc0 = k * 2
                    g_w1[k] = gload(w1c[e_][:, :, hc0 * 128:(hc0 + 2) * 128], (DC, 256))
                    g_w2[k] = gload(w2c[e_][:, :, hc0 * 128:(hc0 + 2) * 128], (DC, 256), q=1)

            def load_w3(dc, e_=e, g_w3=g_w3):
                if dc < DC and g_w3[dc] is None:
                    g_w3[dc] = gload(w3c[e_][:, :, dc * 128:(dc + 1) * 128], (HC, 128), q=dc % 2)

            if e in prefetched:
                g_w1[0], g_w2[0] = prefetched.pop(e)
                load_pair(1)
            else:
                load_pair(0)
                load_pair(1)
            if e < 8:
                wp = e % 2
                wact(d_bco[par], tbo)
                wact(dve_s, web_used[wp])
                inst = ACT.dma_start(
                    out=w_eb[wp][:],
                    in_=wrow_d[par][e:e + 1, :].partition_broadcast(128))
                web_tok[wp] = inc(inst, d_web[wp], 16)

            for hc in range(HC):
                if hc % 2 == 0:
                    if hc // 2 + 2 < 8:
                        load_pair(hc // 2 + 2)
                    elif hc // 2 + 2 == 8:
                        load_w3(0)
                        load_w3(1)
                slot1, slot2 = g_w1[hc // 2], g_w2[hc // 2]
                p1b, p2b = banks[hc % 2], banks[2 + hc % 2]
                p1b.acquire()
                p2b.acquire()
                guse(slot1)
                guse(slot2)
                wpe(dve_s, t_xn)
                off = (hc % 2) * 128
                for dc in range(DC):
                    inst = PE.matmul(
                        p1b.t[:, 0:ROWS],
                        gpool[slot1][:, dc * 256 + off:dc * 256 + off + 128],
                        xnb[:, dc * ROWS:(dc + 1) * ROWS],
                        start=(dc == 0), stop=(dc == DC - 1))
                tp1 = p1b.done(inst)
                for dc in range(DC):
                    inst = PE.matmul(
                        p2b.t[:, 0:ROWS],
                        gpool[slot2][:, dc * 256 + off:dc * 256 + off + 128],
                        xnb[:, dc * ROWS:(dc + 1) * ROWS],
                        start=(dc == 0), stop=(dc == DC - 1))
                tp2 = p2b.done(inst)
                last_p2 = tp2
                if hc % 2 == 1:
                    gdone(slot1, pe_s, tp1)
                    gdone(slot2, pe_s, tp2)
                sp = hc % 2
                wact(pe_s, tp1)
                wact(dve_s, s_guard[sp])
                b1col = b12_sb[:, (e * 2) * HC + hc:(e * 2) * HC + hc + 1]
                tsi = inc(ACT.activation(s_str[sp][:], p1b.t[:, 0:ROWS], AF.Silu,
                                         bias=b1col), act_s)
                p1b.reader(act_s, tsi)
                wdve(pe_s, tp2)
                wdve(act_s, tsi)
                b2col = b12_sb[:, (e * 2 + 1) * HC + hc:(e * 2 + 1) * HC + hc + 1]
                inst = DVE.scalar_tensor_tensor(
                    glu[:, hc * ROWS:(hc + 1) * ROWS], p2b.t[:, 0:ROWS], b2col,
                    s_str[sp][:], ALU.add, ALU.mult)
                t_glu[hc] = s_guard[sp] = inc(inst, dve_s)
                p2b.reader(dve_s, t_glu[hc])
            for dc in range(DC):
                load_w3(dc + 2)
                if dc == 4 and e + 1 < NE:
                    nxt_pair(e + 1)
                slot3 = g_w3[dc]
                wb3 = banks[4 + dc % 2]
                wb3.acquire()
                guse(slot3)
                wpe(dve_s, t_glu[HC - 1])
                for hc in range(HC):
                    inst = PE.matmul(
                        wb3.t[:, 0:ROWS],
                        gpool[slot3][:, hc * 128:(hc + 1) * 128],
                        glu[:, hc * ROWS:(hc + 1) * ROWS],
                        start=(hc == 0), stop=(hc == HC - 1))
                tw3 = wb3.done(inst)
                gdone(slot3, pe_s, tw3)
                wdve(pe_s, tw3)
                if e < 8:
                    wp = e % 2
                    wdve(d_web[wp], web_tok[wp])
                    DVE.tensor_tensor(tmp_str[:], wb3.t[:, 0:ROWS], w_eb[wp][:], ALU.mult)
                    inst = DVE.tensor_tensor(
                        xbuf[:, dc * ROWS:(dc + 1) * ROWS],
                        xbuf[:, dc * ROWS:(dc + 1) * ROWS], tmp_str[:], ALU.add)
                    t_comb = inc(inst, dve_s)
                    web_used[wp] = t_comb
                else:
                    inst = DVE.scalar_tensor_tensor(
                        xbuf[:, dc * ROWS:(dc + 1) * ROWS], wb3.t[:, 0:ROWS], 0.25,
                        xbuf[:, dc * ROWS:(dc + 1) * ROWS], ALU.mult, ALU.add)
                    t_comb = inc(inst, dve_s)
                wb3.reader(dve_s, t_comb)
            if DRAIN_MODE == "interleaved":
                drain(4)
        wrow_free[par] = max(web_tok)
        xn_free_pe[par] = last_p2
        return t_comb

    def emit_p(j):
        seg = j // 2
        half = j % 2
        t0 = j * TP
        parj = j % 2
        ncols_t = min(130, T - t0)
        xb_main = xf[seg % 2]
        if half == 0:
            groups = [(
                (lambda dc, xb=xb_main, b_=b: xb[:, dc * ROWS + b_ * TS:
                                              dc * ROWS + b_ * TS + ncols_t]),
                b * ncols_t, ncols_t) for b in range(2)]
        else:
            m1 = 128
            m2 = ncols_t - m1
            groups = [(
                (lambda dc, xb=xb_main, b_=b: xb[:, dc * ROWS + b_ * TS + 128:
                                              dc * ROWS + b_ * TS + 256]),
                b * ncols_t, m1) for b in range(2)]
            if m2 > 0:
                xb_next = xf[(seg + 1) % 2]
                groups += [(
                    (lambda dc, xb=xb_next, b_=b: xb[:, dc * ROWS + b_ * TS:
                                                  dc * ROWS + b_ * TS + m2]),
                    b * ncols_t + m1, m2) for b in range(2)]
        ncl = 2 * ncols_t
        rb, nb, tbc = emit_ln(groups, ncl, j % 2)
        wdve(pe_s, tbc)
        wdve(pe_s, xn2_free_pe[0])
        inst = None
        for (gslab, coff, cols) in groups:
            for dc in range(DC):
                DVE.tensor_tensor(tmp_str[:, 0:cols], gslab(dc),
                                  rb.t[:, coff:coff + cols], ALU.mult)
                inst = DVE.tensor_tensor(
                    xn2[:, dc * 260 + coff:dc * 260 + coff + cols],
                    tmp_str[:, 0:cols], nb.t[:, coff:coff + cols], ALU.add)
        t_xn2 = inc(inst, dve_s)
        rb.reader(dve_s, t_xn2)
        nb.reader(dve_s, t_xn2)

        wsync(d_c6, t_c6)
        wact(d_c6, t_c6)
        g_win = [None] * 8

        def load_win(k, g_win=g_win):
            if k < 8 and g_win[k] is None:
                g_win[k] = gload(winc[:, :, (k * 2) * 128:(k * 2 + 2) * 128], (DC, 256), q=k % 2)

        load_win(0)
        load_win(1)
        t_conv = {}
        last_xz_pe = 0
        for hcz in range(HC):
            if hcz % 2 == 0:
                load_win(hcz // 2 + 2)
            slotw = g_win[hcz // 2]
            zb = util_bank()
            zb.acquire()
            guse(slotw)
            wpe(dve_s, t_xn2)
            offw = (hcz % 2) * 128
            for dc in range(DC):
                inst = PE.matmul(
                    zb.t[:, 0:ncl],
                    gpool[slotw][:, dc * 256 + offw:dc * 256 + offw + 128],
                    xn2[:, dc * 260:dc * 260 + ncl],
                    start=(dc == 0), stop=(dc == DC - 1))
            tzz = zb.done(inst)
            last_xz_pe = tzz
            if hcz % 2 == 1:
                gdone(slotw, pe_s, tzz)
            rp = hcz % 2
            wdve(pe_s, tzz)
            wdve(dve_s, t_conv.get(hcz - 2, 0))
            for b in range(2):
                DVE.tensor_copy(raw[rp][:, b * 131 + 1:b * 131 + 1 + ncols_t],
                                zb.t[:, b * ncols_t:(b + 1) * ncols_t])
                if j == 0:
                    DVE.memset(raw[rp][:, b * 131:b * 131 + 1], 0.0)
                else:
                    DVE.tensor_copy(raw[rp][:, b * 131:b * 131 + 1],
                                    carry[:, hcz * 2 + b:hcz * 2 + b + 1])
                if ncols_t < 130:
                    DVE.memset(raw[rp][:, b * 131 + 1 + ncols_t:(b + 1) * 131], 0.0)
            tev = inc(DVE.tensor_copy(
                carry.ap().rearrange("p (h b2) -> p h b2", b2=2)[:, hcz, :],
                raw[rp].ap().rearrange("p (b c) -> p b c", c=131)[:, :, 128]), dve_s)
            zb.reader(dve_s, tev)
            xzk = 0 if hcz < 8 else 1
            dcz = hcz % 8
            for b in range(2):
                cp = b
                kbase = (xzk * DC + dcz) * 4
                wdve(act_s, max(ca_guard))
                DVE.tensor_scalar(ca[cp][:], raw[rp][:, b * 131:b * 131 + TP],
                                  kk_sb[:, kbase:kbase + 1], None, ALU.mult)
                for tap in range(1, 4):
                    inst = DVE.scalar_tensor_tensor(
                        ca[cp][:], raw[rp][:, b * 131 + tap:b * 131 + tap + TP],
                        kk_sb[:, kbase + tap:kbase + tap + 1], ca[cp][:],
                        ALU.mult, ALU.add)
                t_cv = inc(inst, dve_s)
                wact(dve_s, t_cv)
                if xzk == 0:
                    wact(dve_s, xs_free_dve[parj])
                    inst = ACT.activation(
                        xs[parj][:, dcz * PR + b * TP:dcz * PR + (b + 1) * TP],
                        ca[cp][:], AF.Silu)
                else:
                    wact(pe_s, z_free_pe[0])
                    inst = ACT.activation(
                        z_bf[:, dcz * ROWS + b * TS + half * TP:
                             dcz * ROWS + b * TS + (half + 1) * TP],
                        ca[cp][:], AF.Silu)
                t_conv[hcz] = ca_guard[cp] = inc(inst, act_s)
        xn2_free_pe[0] = last_xz_pe
        xs_act_tok[parj] = t_conv[7]
        z_act_tok[j] = t_conv[15]

        db = util_bank()
        db.acquire()
        wpe(d_c6, t_c6)
        wdve(act_s, t_conv[7])
        tdb = 0
        for dc in range(DC):
            xp = dc % 2
            wdve(pe_s, max(xsb_guard))
            tcast = inc(DVE.tensor_copy(xsb[xp][:], xs[parj][:, dc * PR:(dc + 1) * PR]),
                        dve_s)
            wpe(dve_s, tcast)
            inst = PE.matmul(db.t[0:96, 0:PR], wxp_sb[:, dc * 96:(dc + 1) * 96],
                             xsb[xp][:], start=(dc == 0), stop=(dc == DC - 1))
            xsb_guard[xp] = tdb = inc(inst, pe_s)
        wact(pe_s, tdb)
        wact(pe_s, dtlow_free_pe[parj])
        ACT.activation(dtlow[parj][:], db.t[0:DTR, 0:PR], AF.Copy)
        wact(d_bco[0], bc_out[0])
        wact(d_bco[1], bc_out[1])
        tb2 = inc(ACT.activation(bcst[parj][:], db.t[64:96, 0:PR], AF.Copy), act_s)
        dtlow_act[parj] = tb2
        db.reader(act_s, tb2)
        bc_out[parj] = inc(ACT.dma_start(out=bc_d[parj], in_=bcst[parj][:]), d_bco[parj], 16)

    def emit_sc(j):
        seg = j // 2
        half = j % 2
        parj = j % 2

        def mk_chunk(b, dc):
            def run():
                dmb = util_bank()
                dmb.acquire()
                wpe(act_s, dtlow_act[parj])
                wpe(d_c6, t_c6)
                inst = PE.matmul(dmb.t[:, 0:PR], wdt_sb[:, dc * 128:(dc + 1) * 128],
                                 dtlow[parj][:], start=True, stop=True)
                tdm = dmb.done(inst)
                dtlow_free_pe[parj] = tdm
                wact(pe_s, tdm)
                wact(dve_s, sp_free[0])
                bcol = bdt_sb[:, dc:dc + 1]
                sl = dmb.t[:, b * TP:(b + 1) * TP]
                ACT.activation(sp_r[:], sl, AF.Relu, bias=bcol)
                tso = inc(ACT.activation(sp_a[:], sl, AF.Abs, bias=bcol), act_s)
                dmb.reader(act_s, tso)
                ACT.activation(sp_a[:], sp_a[:], AF.Exp, scale=-1.0)
                t_sp = inc(ACT.activation(sp_a[:], sp_a[:], AF.Ln, bias=onec[:]), act_s)
                if dc == 0:
                    wact(dve_s, BC_B_users[0])
                    wact(d_bco[parj], bc_out[parj])
                    srcB = bc_d[parj][0:16, b * TP:(b + 1) * TP].unsqueeze(0)
                    inc(ACT.dma_start(
                        out=B_sb.ap().rearrange("p (n c) -> p n c", c=TP),
                        in_=srcB.partition_broadcast(128)), d_bcl, 16)
                    wact(gp_s, BC_C_users[0])
                    srcC = bc_d[parj][16:32, b * TP:(b + 1) * TP].unsqueeze(0)
                    tC = inc(ACT.dma_start(
                        out=C_sb.ap().rearrange("p (n c) -> p n c", c=TP),
                        in_=srcC.partition_broadcast(128)), d_bcl, 16)
                    run.tBC = tC
                wdve(act_s, t_sp)
                DVE.tensor_tensor(delta_s[:], sp_r[:], sp_a[:], ALU.add)
                wdve(act_s, xs_act_tok[parj])
                slx = xs[parj][:, dc * PR + b * TP:dc * PR + (b + 1) * TP]
                tdu = inc(DVE.tensor_tensor(du_s[:], delta_s[:], slx, ALU.mult), dve_s)
                sp_free[0] = tdu
                wdve(d_bcl, getattr(run, "tBC", 0))
                wdve(gp_s, hall_gp[0])
                t_scan = 0
                for n in range(NST):
                    ab = a_pool[n % NA]
                    acol = A_sb[:, dc * NST + n:dc * NST + n + 1]
                    ta = inc(DVE.scalar_tensor_tensor(
                        ab[:], delta_s[:], acol, delta_s[:], ALU.mult, ALU.bypass), dve_s)
                    wact(dve_s, ta)
                    tae = inc(ACT.activation(ab[:], ab[:], AF.Exp), act_s)
                    dbb = dbu_b[n % 2]
                    DVE.tensor_tensor(dbb[:], du_s[:], B_sb[:, n * TP:(n + 1) * TP],
                                      ALU.mult)
                    wdve(act_s, tae)
                    hini = h_fin[:, dc * NST * 2 + n * 2 + b:
                                 dc * NST * 2 + n * 2 + b + 1]
                    inst = DVE.tensor_tensor_scan(
                        h_all[:, n * TP:(n + 1) * TP], ab[:], dbb[:], hini,
                        ALU.mult, ALU.add)
                    t_scan = inc(inst, dve_s)
                tfin = inc(DVE.tensor_copy(
                    h_fin.ap().rearrange("p (d2 n b2) -> p d2 n b2", n=NST, b2=2)[:, dc, :, b],
                    h_all.ap().rearrange("p (n c) -> p n c", c=TP)[:, :, TP - 1]), dve_s)
                BC_B_users[0] = t_scan
                wgp(dve_s, tfin)
                wgp(d_bcl, getattr(run, "tBC", 0))
                GP.tensor_tensor(h_all[:], h_all[:], C_sb[:], ALU.mult)
                v = h_all.ap().rearrange("p (n c) -> p n c", c=TP)
                GP.tensor_tensor(v[:, 0:8, :], v[:, 0:8, :], v[:, 8:16, :], ALU.add)
                GP.tensor_tensor(v[:, 0:4, :], v[:, 0:4, :], v[:, 4:8, :], ALU.add)
                GP.tensor_tensor(v[:, 0:2, :], v[:, 0:2, :], v[:, 2:4, :], ALU.add)
                tg = GP.tensor_tensor(v[:, 0:1, :], v[:, 0:1, :], v[:, 1:2, :], ALU.add)
                wdve(gp_s, tg)
                wdve(pe_s, y_free_pe[0])
                ydst = y_bf[:, dc * ROWS + b * TS + half * TP:
                            dc * ROWS + b * TS + (half + 1) * TP]
                ty = DVE.scalar_tensor_tensor(
                    ydst, slx, dp_sb[:, dc:dc + 1], h_all[:, 0:TP],
                    ALU.mult, ALU.add)
                hall_gp[0] = tg
                BC_C_users[0] = tg
                y_dve_tok[j] = ty
                xs_free_dve[parj] = ty
                if b == 1 and dc == DC - 1:
                    sc_done[j] = True
            return run

        for b in range(2):
            for dc in range(DC):
                ch = mk_chunk(b, dc)
                if DRAIN_MODE == "immediate":
                    ch()
                else:
                    pending.append(ch)

    def emit_e(i):
        if not (sc_done[2 * i] and sc_done[2 * i + 1]):
            drain(10 ** 9)
        assert sc_done[2 * i] and sc_done[2 * i + 1], f"SC chunks for E{i} not drained"
        wact(d_sp, spill_tok[i])
        wact(d_out, ebuf_free[0])
        wact(d_tap, d_tap.count)
        reload_tok[i] = inc(ACT.dma_start(out=xe[:], in_=spill_d[i]), d_rl, 16)
        wsync(d_c6, t_c6)
        wact(d_c6, t_c6)
        g_wo = [None] * DC

        def load_wo(dc, g_wo=g_wo):
            if dc < DC and g_wo[dc] is None:
                g_wo[dc] = gload(woutc[:, :, dc * 128:(dc + 1) * 128], (HC, 128), q=dc % 2)

        load_wo(0)
        load_wo(1)
        wpe(dve_s, y_dve_tok[2 * i + 1])
        wpe(act_s, z_act_tok[2 * i + 1])
        wdve(d_rl, reload_tok[i])
        two = 0
        for dco in range(DC):
            load_wo(dco + 2)
            wb = banks[4 + dco % 2]
            wb.acquire()
            slotw = g_wo[dco]
            guse(slotw)
            for kc in range(HC):
                rhs = y_bf if kc < 8 else z_bf
                dcs = kc % 8
                inst = PE.matmul(
                    wb.t[:, 0:ROWS],
                    gpool[slotw][:, kc * 128:(kc + 1) * 128],
                    rhs[:, dcs * ROWS:(dcs + 1) * ROWS],
                    start=(kc == 0), stop=(kc == HC - 1))
            two = wb.done(inst)
            gdone(slotw, pe_s, two)
            wdve(pe_s, two)
            inst = DVE.scalar_tensor_tensor(
                xe[:, dco * ROWS:(dco + 1) * ROWS], wb.t[:, 0:ROWS],
                bout_sb[:, dco:dco + 1],
                xe[:, dco * ROWS:(dco + 1) * ROWS], ALU.add, ALU.add)
            wb.reader(dve_s, inc(inst, dve_s))
        y_free_pe[0] = two
        z_free_pe[0] = two
        t_done = emit_ffn(i, xe, "E")
        wact(dve_s, t_done)
        for b in range(2):
            dst = out[b, :, i * TS:(i + 1) * TS].rearrange("(dc p) t -> p dc t", p=128)
            src = xe.ap().rearrange("p (dc b t) -> p dc b t", b=2, t=TS)[:, :, b, :]
            out_tok[i] = inc(ACT.dma_start(out=dst, in_=src), d_out, 16)
        ebuf_free[0] = out_tok[i]

    def emit_x(i):
        wact(dve_s, xseg_done[i])
        spill_tok[i] = inc(ACT.dma_start(out=spill_d[i], in_=xf[i % 2][:]), d_sp, 16)
        wact(d_sp, spill_tok[i])
        if i + 2 < NSEG:
            load_x(i + 2)

    def emit_a(i):
        wdve(d_in[i % 2], x_load_tok[i])
        emit_ffn(i, xf[i % 2], "A")

    sched = _sched()
    if STAGE_LIMIT is not None:
        sched = sched[:STAGE_LIMIT]
    for stg in sched:
        if stg.startswith("SC"):
            emit_sc(int(stg[2:]))
        elif stg[0] == "H":
            k = int(stg[1:])
            wdve(d_in[k % 2], x_load_tok[k])
            emit_fhead(k, xf[k % 2], "A")
        elif stg[0] == "A":
            emit_a(int(stg[1:]))
        elif stg[0] == "P":
            j = int(stg[1:])
            emit_p(j)
            if j % 2 == 1:
                xseg_done[j // 2] = dve_s.count
        elif stg[0] == "E":
            emit_e(int(stg[1:]))
        elif stg[0] == "X":
            emit_x(int(stg[1:]))
        if stg in taps:
            drain(10 ** 9)
            src = xe if stg[0] == "E" else xf[int(stg[1:]) % 2]
            wact(dve_s, dve_s.count)
            wact(gp_s, gp_s.count)
            inc(ACT.dma_start(out=taps[stg], in_=src[:]), d_tap, 16)

    drain(10 ** 9)
    for name in DEBUG_DUMP:
        t, shp, dt = sreg[name]
        dd = nc.dram_tensor("dump_" + name, shp, dt, kind="ExternalOutput").ap()
        wact(dve_s, dve_s.count)
        wact(act_s, act_s.count)
        wact(gp_s, gp_s.count)
        wact(pe_s, pe_s.count)
        inc(ACT.dma_start(out=dd, in_=t[:]), d_tap, 16)
    for s_ in sems:
        if s_.name.startswith("d_"):
            wsync(s_, s_.count)

    st.close()
    for s_ in sems:
        s_._cm.__exit__(None, None, None)
    return nc


def _host_prep(inputs):
    x = np.ascontiguousarray(inputs["L0_attn_x"])
    f32 = np.float32
    w1 = np.concatenate([inputs["L0_ffn_rw1"], inputs["L0_ffn_sw1"]], 0).reshape(NE, DC, 128, H)
    w2 = np.concatenate([inputs["L0_ffn_rw2"], inputs["L0_ffn_sw2"]], 0).reshape(NE, DC, 128, H)
    w3 = np.concatenate([inputs["L0_ffn_rw3"], inputs["L0_ffn_sw3"]], 0).reshape(NE, HC, 128, D)
    win = np.asarray(inputs["L0_attn_W_in"]).reshape(DC, 128, H)
    wout = np.asarray(inputs["L0_attn_W_out"]).reshape(HC, 128, D)
    wxp = np.asarray(inputs["L0_attn_W_xp"]).reshape(DC, 128, 96)
    wdt = np.ascontiguousarray(inputs["L0_attn_W_dt"])
    gw = np.asarray(inputs["L0_ffn_gate_w"]).reshape(DC, 128, 8)
    b1 = np.concatenate([inputs["L0_ffn_rb1"], inputs["L0_ffn_sb1"]], 0)
    b2 = np.concatenate([inputs["L0_ffn_rb2"], inputs["L0_ffn_sb2"]], 0)
    b12 = np.stack([b1.reshape(NE, HC, 128), b2.reshape(NE, HC, 128)], 1)
    b12 = np.ascontiguousarray(b12.transpose(3, 0, 1, 2)).reshape(128, NE * 2 * HC)
    b3 = np.concatenate([inputs["L0_ffn_rb3"], inputs["L0_ffn_sb3"]], 0)
    bdt = np.ascontiguousarray(np.asarray(inputs["L0_attn_b_dt"]).reshape(DC, 128).T)
    alog = np.ascontiguousarray(
        np.asarray(inputs["L0_attn_A_log"]).reshape(DC, 128, NST).transpose(1, 0, 2)
    ).reshape(128, DC * NST)
    dp = np.ascontiguousarray(np.asarray(inputs["L0_attn_D_param"]).reshape(DC, 128).T)
    bout = np.ascontiguousarray(np.asarray(inputs["L0_attn_b_out"]).reshape(DC, 128).T)
    kx = np.asarray(inputs["L0_attn_K_x"])[:, 0, :].reshape(4, DC, 128)
    kz = np.asarray(inputs["L0_attn_K_z"])[:, 0, :].reshape(4, DC, 128)
    kk = np.stack([kx, kz], 0)
    kk = np.ascontiguousarray(kk.transpose(3, 0, 2, 1)).reshape(128, 2 * DC * 4)
    gb = np.asarray(inputs["L0_ffn_gate_b"]).reshape(8, 1)
    eye = np.eye(128, dtype=f32)

    shared = dict(w1f=w1, w2f=w2, w3f=w3, winf=win, woutf=wout, wxpf=wxp,
                  wdtf=wdt, gwf=gw, b12=b12, b3=b3, bdt=bdt, alog=alog,
                  dp=dp, bout=bout, kk=kk, gb=gb, eye=eye)
    shared = {k: np.ascontiguousarray(np.asarray(v, dtype=f32)) for k, v in shared.items()}

    in_maps = []
    for c in range(8):
        m = dict(shared)
        m["x"] = np.ascontiguousarray(
            np.asarray(x[2 * c:2 * c + 2], dtype=f32).transpose(0, 2, 1))
        in_maps.append(m)
    return in_maps


def kernel(**inputs):
    in_maps = _host_prep(inputs)
    nc = build()
    res = run_bass_kernel_spmd(nc, in_maps, core_ids=list(range(8)))
    outs = [res.results[c]["out"].transpose(0, 2, 1) for c in range(8)]
    return np.ascontiguousarray(np.concatenate(outs, 0)).astype(np.float32)
